# version 22
# baseline (speedup 1.0000x reference)
import numpy as np
import ml_dtypes
import concourse.bass as bass
import concourse.mybir as mybir
from concourse.bass_utils import run_bass_kernel_spmd

F32 = mybir.dt.float32
BF16 = mybir.dt.bfloat16
AF = mybir.ActivationFunctionType
ALU = mybir.AluOpType
bfnp = ml_dtypes.bfloat16

NCORE = 8
P = 128
T = 1024
NT = 8
D = 2048
ND = 16
NH = 8
DK = 256
DV = 512
CH = 128
NCH = T // CH
HPG = 1
NG = NH // HPG
GAMMA = [float(1.0 - 2.0 ** (-5 - h)) for h in range(NH)]
MAGIC = 12582912.0
INTER = 5632
NSU = INTER // P
HF = 8192
NHF = HF // P
ENGS = ("sync", "scalar", "vector", "tensor", "gpsimd")
DMA_SEMS = ("dw", "da", "out")
DEBUG_BUILD = False
_last_results = None


_slot_n = [0]


class Slot:
    __slots__ = ("w", "r", "ds")

    def __init__(self):
        self.w = None
        self.r = None
        self.ds = f"dma{_slot_n[0]}"
        _slot_n[0] += 1


def build_program(debug=False):
    nc = bass.Bass("TRN2", target_bir_lowering=False, debug=False)
    scratch_kind = "ExternalOutput" if debug else None

    def dram(name, shape, dtype, kind=None):
        if kind is None:
            return nc.dram_tensor(name, shape, dtype)
        return nc.dram_tensor(name, shape, dtype, kind=kind)

    x0fm = dram("x0fm", [D, T], F32, "ExternalInput").ap()
    qw = dram("qw", [D, D], F32, "ExternalInput").ap()
    kw = dram("kw", [D, D], F32, "ExternalInput").ap()
    vw = dram("vw", [D, 2 * D], F32, "ExternalInput").ap()
    gw = dram("gw", [D, 2 * D], F32, "ExternalInput").ap()
    ow = dram("ow", [2 * D, D], F32, "ExternalInput").ap()
    gatew = dram("gatew", [D, 2 * INTER], F32, "ExternalInput").ap()
    downw = dram("downw", [INTER, D], F32, "ExternalInput").ap()
    wgu_sh = dram("wgu_sh", [D, D], F32, "ExternalInput").ap()
    wp_sh = dram("wp_sh", [HF, 256], F32, "ExternalInput").ap()
    cosraw = dram("cosraw", [P, T], F32, "ExternalInput").ap()
    sinraw = dram("sinraw", [P, T], F32, "ExternalInput").ap()
    attw_i = dram("attw_i", [P, ND], F32, "ExternalInput").ap()
    mlpw_i = dram("mlpw_i", [P, ND], F32, "ExternalInput").ap()
    gnw_i = dram("gnw_i", [P, 4], F32, "ExternalInput").ap()
    dmaskT_i = dram("dmaskT_i", [P, NH * CH], BF16, "ExternalInput").ap()
    cdecT_i = dram("cdecT_i", [P, NH * CH], BF16, "ExternalInput").ap()
    kdec_i = dram("kdec_i", [P, NH], F32, "ExternalInput").ap()
    gamtab_i = dram("gamtab_i", [NH * P, T], BF16, "ExternalInput").ap()
    wcomb_i = dram("wcomb_i", [P, 64], F32, "ExternalInput").ap()
    eye_i = dram("eye_i", [P, P], BF16, "ExternalInput").ap()
    onesb_i = dram("onesb_i", [P, 2], BF16, "ExternalInput").ap()
    onesc_i = dram("onesc_i", [P, 1], BF16, "ExternalInput").ap()
    onesr_i = dram("onesr_i", [1, P], BF16, "ExternalInput").ap()

    y_out = dram("y", [D, T], F32, "ExternalOutput").ap()

    a_scr = dram("a_scr", [ND * P, T], BF16, scratch_kind).ap()
    q_scr = dram("q_scr", [ND * P, T], BF16, scratch_kind).ap()
    o_scr = dram("o_scr", [32 * P, T], BF16, scratch_kind).ap()
    sg_scr = dram("sg_scr", [32 * P, T], BF16, scratch_kind).ap()
    res2_scr = dram("res2_scr", [ND * P, T], F32, scratch_kind).ap()
    u3_scr = dram("u3_scr", [ND * P, T], BF16, scratch_kind).ap()
    x1_scr = dram("x1_scr", [ND * P, T], F32, scratch_kind).ap()
    sloc_scr = dram("sloc_scr", [2 * NH * P, DV], BF16).ap()
    g1out = dram("g1out", [4 * 2 * NH * P, DV], BF16).ap()
    wqgu_st = dram("wqgu_st", [D, D], BF16).ap()
    wqp_st = dram("wqp_st", [HF, 256], BF16).ap()
    g2gu = nc.dram_tensor("g2gu", [NCORE * D, D], BF16, addr_space="Shared").ap()
    g2p = nc.dram_tensor("g2p", [NCORE * HF, 256], BF16, addr_space="Shared").ap()
    rowdr = dram("rowdr", [8, T], F32).ap()
    rowbf = dram("rowbf", [4, 2048], BF16).ap()

    from collections import defaultdict
    Q = {e: [] for e in ENGS}
    cnt = defaultdict(int)
    lastw = {e: {} for e in ENGS}

    def is_dma_sem(s):
        return s.startswith("dma") or s in ("out", "dtab")

    def em(eng, fn, inc=None):
        step = None
        if inc is not None:
            step = 16 if is_dma_sem(inc) else 1
            cnt[inc] += step
        Q[eng].append(("op", fn, inc, step))
        return cnt[inc] if inc else None

    def wt(eng, sem, val=None):
        v = cnt[sem] if val is None else val
        if v is None or v <= 0 or v <= lastw[eng].get(sem, 0):
            return
        lastw[eng][sem] = v
        Q[eng].append(("wait", sem, v))

    def use_w(eng, *slots):
        for s in slots:
            if s.r is not None:
                wt(eng, *s.r)
            if s.w is not None:
                wt(eng, *s.w)

    def use_r(eng, *slots):
        for s in slots:
            if s.w is not None:
                wt(eng, *s.w)

    def mark_w(sem, *slots):
        for s in slots:
            s.w = (sem, cnt[sem])

    def mark_r(sem, *slots):
        for s in slots:
            s.r = (sem, cnt[sem])

    def barrier():
        for e in ENGS:
            em(e, lambda E: E.nop(), inc="b_" + e)
        vals = {e: cnt["b_" + e] for e in ENGS}
        for e in ENGS:
            for o in ENGS:
                if o != e:
                    wt(e, "b_" + o, vals[o])

    from contextlib import ExitStack
    es = ExitStack()

    def sb(name, shape, dtype):
        return es.enter_context(nc.sbuf_tensor(name, shape, dtype))

    BIG = sb("BIG", [P, 32768], BF16)
    QG = sb("QG", [P, 2048], BF16)
    KG = sb("KG", [P, 2048], BF16)
    OSC = sb("OSC", [P, 4096], BF16)
    SX = sb("SX", [P, 1024], F32)
    SBX = sb("SBX", [P, 1024], BF16)
    WS = [sb(f"WS{i}", [P, 4096], BF16) for i in range(4)]
    SCR = [sb(f"SCR{i}", [P, 1024], F32) for i in range(5)]
    BC = [sb(f"BC{i}", [P, 1024], F32) for i in range(3)]
    OS = [sb(f"OS{i}", [P, 1024], BF16) for i in range(6)]
    SBXB = sb("SBXB", [P, 1024], BF16)
    SXB_is_BC1 = True
    TCK = sb("TCK", [P, T], BF16)
    TSK = sb("TSK", [P, T], BF16)
    DMK = sb("DMK", [P, NH * CH], BF16)
    CDT = sb("CDT", [P, NH * CH], BF16)
    KDC = sb("KDC", [P, NH], F32)
    ATW = sb("ATW", [P, ND], F32)
    MLW = sb("MLW", [P, ND], F32)
    GNW = sb("GNW", [P, 4], F32)
    WCB = sb("WCB", [P, 64], F32)
    EYE = sb("EYE", [P, P], BF16)
    ONB = sb("ONB", [P, 2], BF16)
    ONC = sb("ONC", [P, 1], BF16)
    ONR = sb("ONR", [1, P], BF16)
    STC = sb("STC", [P, 2 * NT], F32)
    ROW = sb("ROW", [1, 4096], F32)
    RWB = sb("RWB", [1, 1024], BF16)
    QWB = [sb(f"QWB{i}", [P, 1024], BF16) for i in range(3)]
    QSB = sb("QSB", [P, 1024], BF16)
    QEX = [sb(f"QEX{i}", [P, 1024], BF16) for i in range(2)]
    QRB = sb("QRB", [2, 1024], BF16)
    MGP = sb("MGP", [P, 1], F32)
    MGN = sb("MGN", [P, 1], F32)

    PS = [es.enter_context(nc.psum_tensor(f"PS{i}", [P, 512], F32))
          for i in range(6)]
    PTR = es.enter_context(nc.psum_tensor("PTR", [P, 512], BF16))
    PSR = es.enter_context(nc.psum_tensor("PSR", [P, 512], F32))
    PRA = PSR
    PRB = PS[5]

    sBIG = Slot(); sQG = Slot(); sKG = Slot(); sOSC = Slot()
    sSX = Slot(); sSBX = Slot(); sSXB = Slot(); sSBXB = Slot()
    sW = [Slot() for _ in WS]
    wrr = [0]

    def next_w():
        i = wrr[0] % 4
        wrr[0] += 1
        return i

    prr = [0]

    def next_ps():
        i = prr[0] % 4
        prr[0] += 1
        return i
    sSCR = [Slot() for _ in SCR]
    sBC = [Slot() for _ in BC]
    sOS = [Slot() for _ in OS]
    sPS = [Slot() for _ in PS]
    sPTR = Slot(); sPSR = Slot()
    sPRA = sPSR
    sPRB = sPS[5]
    sROW = Slot(); sRWB = Slot(); sSTC = Slot(); sTAB = Slot()
    sQW = [Slot(), Slot(), Slot()]; sQS = Slot()
    sQE = [Slot(), Slot()]
    sQRB = Slot(); sMG = Slot()
    sRD = [Slot() for _ in range(8)]
    sRB = [Slot() for _ in range(4)]
    sDR = {n: Slot() for n in ("a", "q", "o", "sg", "res2", "u3", "x1",
                               "sloc", "wqgu", "wqp")}

    def dma(qeng, out_ap, in_ap, sem=None, oslot=None, islots=()):
        if oslot is not None:
            sem = oslot.ds
        assert sem is not None
        for s in islots:
            if s.w is not None:
                wt(qeng, *s.w)
        if oslot is not None:
            use_w(qeng, oslot)
        em(qeng, lambda E, o=out_ap, i=in_ap: E.dma_start(out=o, in_=i), inc=sem)
        if oslot is not None:
            mark_w(sem, oslot)
        for s in islots:
            s.r = (sem, cnt[sem])

    def expand_row(src_row_ap, n, dst_ap, dst_slot, rd_i):
        dma("gpsimd", rowdr[rd_i:rd_i + 1, 0:n], src_row_ap, "da",
            oslot=sRD[rd_i], islots=(sROW,))
        src = rowdr[rd_i:rd_i + 1, 0:n].rearrange("r c -> r () c") \
            .broadcast_to([1, P, n])
        dma("gpsimd", dst_ap, src, "da", oslot=dst_slot, islots=(sRD[rd_i],))

    def mm(out_ap, lhsT_ap, rhs_ap, start, stop, inc=None):
        em("tensor",
           lambda E, o=out_ap, l=lhsT_ap, r=rhs_ap, s1=start, s2=stop:
               E.matmul(o, l, r, start=s1, stop=s2), inc=inc)

    def load_panel(wi, dram_ap, r0, r1, c0, c1, qeng="gpsimd"):
        width = c1 - c0
        nk = (r1 - r0) // P
        assert nk * width <= 4096
        src = dram_ap[r0:r1, c0:c1].rearrange("(t p) c -> p t c", p=P)
        dma(qeng, WS[wi][:, 0:nk * width], src, oslot=sW[wi])

    def wtile(wi, k, width, o0, ow_):
        return WS[wi][:, k * width + o0:k * width + o0 + ow_]

    def row_rstd(src_off, dst_off, n=512):
        wt("scalar", "dv")
        em("scalar", lambda E: E.activation(ROW[0:1, src_off:src_off + n],
                                            ROW[0:1, src_off:src_off + n],
                                            AF.Ln))
        em("scalar", lambda E: E.activation(ROW[0:1, dst_off:dst_off + n],
                                            ROW[0:1, src_off:src_off + n],
                                            AF.Exp, scale=-0.5), inc="ac")
        mark_w("ac", sROW)

    for dst, srcp in ((DMK, dmaskT_i), (CDT, cdecT_i), (KDC, kdec_i),
                      (ATW, attw_i), (MLW, mlpw_i), (GNW, gnw_i),
                      (WCB, wcomb_i), (EYE, eye_i), (ONB, onesb_i),
                      (ONC, onesc_i), (ONR, onesr_i)):
        dma("sync", dst[:], srcp, "dtab")
    tab_dw = cnt["dtab"]
    em("vector", lambda E: E.memset(MGP[:], MAGIC))
    em("vector", lambda E: E.memset(MGN[:], -MAGIC), inc="dv")
    mark_w("dv", sMG)

    for t in range(ND):
        sc = sSCR[t % 2]
        dma("sync", SCR[t % 2][:], x0fm[t * P:(t + 1) * P, :], "da", oslot=sc)
        use_r("scalar", sc)
        use_w("scalar", sOS[t % 2], sOS[2 + t % 2])
        em("scalar", lambda E, t=t: E.copy(OS[t % 2][:], SCR[t % 2][:]))
        em("scalar", lambda E, t=t: E.activation(OS[2 + t % 2][:], OS[t % 2][:],
                                                 AF.Square), inc="ac")
        mark_w("ac", sOS[t % 2], sOS[2 + t % 2])
        sc.r = ("ac", cnt["ac"])
        wt("tensor", "dtab", tab_dw)
        use_r("tensor", sOS[t % 2], sOS[2 + t % 2])
        if t == 0:
            use_w("tensor", sPRA, sPRB, sPS[3], sPS[4])
        for sl in range(2):
            pr = (PRA, PRB)[sl]
            mm(pr[0:1, :], ONC[:], OS[t % 2][:, sl * 512:(sl + 1) * 512],
               start=(t == 0), stop=(t == ND - 1))
            pq = (PS[3], PS[4])[sl]
            mm(pq[0:1, :], ONC[:], OS[2 + t % 2][:, sl * 512:(sl + 1) * 512],
               start=(t == 0), stop=(t == ND - 1),
               inc="pe" if sl == 1 else None)
        mark_r("pe", sOS[t % 2], sOS[2 + t % 2])
    mark_w("pe", sPRA, sPRB, sPS[3], sPS[4])

    use_r("vector", sPRA, sPRB, sPS[3], sPS[4])
    use_w("vector", sROW)
    for sl in range(2):
        em("vector", lambda E, sl=sl, pr=(PRA, PRB)[sl]: E.tensor_scalar(
            ROW[0:1, sl * 512:(sl + 1) * 512], pr[0:1, :], 1.0 / D, None,
            ALU.mult))
        em("vector", lambda E, sl=sl, pq=(PS[3], PS[4])[sl]: E.tensor_scalar(
            ROW[0:1, 1024 + sl * 512:1024 + (sl + 1) * 512], pq[0:1, :],
            1.0 / D, None, ALU.mult))
    em("vector", lambda E: E.tensor_tensor(
        ROW[0:1, 3072:4096], ROW[0:1, 0:1024], ROW[0:1, 0:1024], ALU.mult))
    em("vector", lambda E: E.tensor_sub(
        ROW[0:1, 3072:4096], ROW[0:1, 1024:2048], ROW[0:1, 3072:4096]))
    em("vector", lambda E: E.tensor_scalar(
        ROW[0:1, 2048:3072], ROW[0:1, 3072:4096], 1e-5, None, ALU.add), inc="dv")
    mark_w("dv", sROW)
    mark_r("dv", sPRA, sPRB, sPS[3], sPS[4])
    wt("scalar", "dv")
    em("scalar", lambda E: E.activation(ROW[0:1, 2048:3072],
                                        ROW[0:1, 2048:3072], AF.Ln))
    em("scalar", lambda E: E.activation(ROW[0:1, 1024:2048],
                                        ROW[0:1, 2048:3072], AF.Exp,
                                        scale=-0.5))
    em("scalar", lambda E: E.activation(ROW[0:1, 2048:3072],
                                        ROW[0:1, 2048:3072], AF.Exp,
                                        scale=-1.0), inc="ac")
    mark_w("ac", sROW)
    wt("vector", "ac")
    em("vector", lambda E: E.tensor_tensor(
        ROW[0:1, 2048:3072], ROW[0:1, 3072:4096], ROW[0:1, 2048:3072], ALU.mult))
    em("vector", lambda E: E.tensor_scalar(
        ROW[0:1, 2048:3072], ROW[0:1, 2048:3072], 1e-6, None, ALU.add), inc="dv")
    mark_w("dv", sROW)
    row_rstd(2048, 2048, 1024)
    wt("vector", "ac")
    em("vector", lambda E: E.tensor_tensor(
        ROW[0:1, 2048:3072], ROW[0:1, 1024:2048], ROW[0:1, 2048:3072], ALU.mult))
    expand_row(ROW[0:1, 2048:3072], 1024, BC[0][:], sBC[0], 0)

    dma("sync", SCR[2][:], cosraw, "da", oslot=sSCR[2])
    use_r("vector", sSCR[2], sBC[0])
    use_w("vector", sTAB)
    em("vector", lambda E: E.tensor_tensor(TCK[:], SCR[2][:], BC[0][:], ALU.mult),
       inc="dv")
    sSCR[2].r = ("dv", cnt["dv"])
    dma("sync", SCR[2][:], sinraw, "da", oslot=sSCR[2])
    use_r("vector", sSCR[2])
    em("vector", lambda E: E.tensor_tensor(TSK[:], SCR[2][:], BC[0][:], ALU.mult),
       inc="dv")
    mark_w("dv", sTAB)
    sSCR[2].r = ("dv", cnt["dv"])

    use_r("vector", sROW)
    use_w("vector", sRWB)
    em("vector", lambda E: E.tensor_copy(RWB[0:1, :], ROW[0:1, 2048:3072]),
       inc="dv")
    mark_w("dv", sRWB)
    use_r("tensor", sRWB)
    use_w("tensor", sPTR)
    for tt in range(NT):
        em("tensor", lambda E, tt=tt: E.transpose(
            PTR[:, 2 * tt:2 * tt + 1], RWB[0:1, tt * P:(tt + 1) * P],
            EYE[0:1, 0:1]), inc="pe" if tt == NT - 1 else None)
    mark_w("pe", sPTR)
    mark_r("pe", sRWB)
    use_r("vector", sPTR)
    use_w("vector", sSTC)
    em("vector", lambda E: E.tensor_copy(STC[:], PTR[:, 0:2 * NT]), inc="dv")
    mark_w("dv", sSTC)
    sPTR.r = ("dv", cnt["dv"])

    expand_row(ROW[0:1, 0:1024], 1024, BC[1][:], sBC[1], 2)
    expand_row(ROW[0:1, 1024:2048], 1024, BC[2][:], sBC[2], 3)

    for t in range(ND):
        sc = sSCR[t % 2]
        dma("sync", SCR[t % 2][:], x0fm[t * P:(t + 1) * P, :], "da", oslot=sc)
        use_r("vector", sc, sBC[1], sBC[2])
        use_w("vector", sSCR[2 + t % 2], sOS[t % 2], sBIG)
        em("vector", lambda E, t=t: E.tensor_sub(
            SCR[2 + t % 2][:], SCR[t % 2][:], BC[1][:]))
        em("vector", lambda E, t=t: E.tensor_tensor(
            OS[t % 2][:], SCR[2 + t % 2][:], BC[2][:], ALU.mult))
        em("vector", lambda E, t=t: E.tensor_scalar(
            BIG[:, t * 1024:(t + 1) * 1024], SCR[2 + t % 2][:],
            ATW[:, t:t + 1], None, ALU.mult), inc="dv")
        mark_w("dv", sOS[t % 2], sBIG, sSCR[2 + t % 2])
        sc.r = ("dv", cnt["dv"])
        dma("scalar", a_scr[t * P:(t + 1) * P, :], OS[t % 2][:], "da",
            oslot=sDR["a"], islots=(sOS[t % 2],))

    barrier()

    def quant_segA_load(src_ap, wb):
        dma("gpsimd", QWB[wb][:], src_ap, oslot=sQW[wb])

    def quant_segA(wb, eb, wsl):
        use_r("scalar", sQW[wb])
        use_w("scalar", sQE[eb])
        em("scalar", lambda E, wb=wb, eb=eb: E.activation(
            QEX[eb][:], QWB[wb][:], AF.Abs), inc="ac")
        mark_w("ac", sQE[eb])
        use_r("tensor", sQE[eb])
        nsl = 1024 // wsl
        for sl in range(nsl):
            use_w("tensor", sPS[4])
            mm(PS[4][0:2, 0:wsl], ONB[:], QEX[eb][:, sl * wsl:(sl + 1) * wsl],
               start=True, stop=True, inc="pe")
            mark_w("pe", sPS[4])
            use_r("vector", sPS[4])
            use_w("vector", sQRB)
            em("vector", lambda E, sl=sl, w=wsl: E.tensor_scalar(
                QRB[0:2, sl * w:(sl + 1) * w], PS[4][0:2, 0:w], 1.0 / 64.0,
                1e-8, ALU.mult, ALU.max), inc="dv")
            mark_w("dv", sQRB)
            mark_r("dv", sPS[4])
        mark_r("pe", sQE[eb])
        dma("scalar", rowbf[0:2, 0:1024], QRB[0:2, 0:1024], oslot=sRB[0],
            islots=(sQRB,))
        brd = rowbf[0:2, 0:1024].rearrange("g c -> g () c") \
            .broadcast_to([2, 64, 1024])
        dma("scalar", QEX[eb][:], brd, oslot=sQE[eb], islots=(sRB[0],))

    def quant_segB1(eb):
        use_r("vector", sQE[eb], sMG)
        use_w("vector", sBC[1])
        em("vector", lambda E, eb=eb: E.reciprocal(BC[1][:], QEX[eb][:]),
           inc="dv")
        mark_w("dv", sBC[1])

    def quant_segB2(stage_ap, wb, eb, wname):
        b = wb
        use_r("gpsimd", sQW[b], sBC[1])
        use_w("gpsimd", sQS)
        em("gpsimd", lambda E, b=b: E.tensor_tensor(
            QSB[:], QWB[b][:], BC[1][:], ALU.mult), inc="gp")
        mark_w("gp", sQS)
        sBC[1].r = ("gp", cnt["gp"])
        use_r("scalar", sQS, sMG)
        use_w("scalar", sBC[2])
        em("scalar", lambda E: E.activation(BC[2][:], QSB[:], AF.Relu,
                                            bias=MGP[:]), inc="ac")
        mark_w("ac", sBC[2])
        use_r("vector", sBC[2])
        use_w("vector", sQS)
        em("vector", lambda E: E.tensor_scalar(
            QSB[:], BC[2][:], -MAGIC, 1.0, ALU.add, ALU.min))
        em("vector", lambda E: E.tensor_scalar(
            QSB[:], QSB[:], -1.0, None, ALU.max), inc="dv")
        mark_w("dv", sQS)
        sBC[2].r = ("dv", cnt["dv"])
        use_r("gpsimd", sQS, sQE[eb])
        use_w("gpsimd", sQW[b])
        em("gpsimd", lambda E, b=b, eb=eb: E.tensor_tensor(
            QWB[b][:], QSB[:], QEX[eb][:], ALU.mult), inc="gp")
        mark_w("gp", sQW[b])
        sQS.r = ("gp", cnt["gp"])
        sQE[eb].r = ("gp", cnt["gp"])
        dma("scalar", stage_ap, QWB[b][:], oslot=sDR[wname], islots=(sQW[b],))

    qspecs = []
    for rt in range(ND):
        for cc in range(2):
            qspecs.append((wgu_sh[rt * P:(rt + 1) * P,
                                  cc * 1024:(cc + 1) * 1024],
                           wqgu_st[rt * P:(rt + 1) * P,
                                   cc * 1024:(cc + 1) * 1024], 512, "wqgu"))
    for bt in range(ND):
        qspecs.append((wp_sh[bt * 512:(bt + 1) * 512, 0:256]
                       .rearrange("(t p) c -> p t c", p=P),
                       wqp_st[bt * 512:(bt + 1) * 512, 0:256]
                       .rearrange("(t p) c -> p t c", p=P), 256, "wqp"))
    qstate = {"a": 0, "b1": 0, "b2": 0}

    def pop_quant(n):
        for _ in range(n):
            if qstate["a"] < len(qspecs):
                quant_segA_load(qspecs[qstate["a"]][0], qstate["a"] % 3)
            if qstate["b2"] < qstate["b1"]:
                i = qstate["b2"]
                quant_segB2(qspecs[i][1], i % 3, i % 2, qspecs[i][3])
                qstate["b2"] += 1
            if qstate["b1"] < qstate["a"]:
                i = qstate["b1"]
                quant_segB1(i % 2)
                qstate["b1"] += 1
            if qstate["a"] < len(qspecs):
                i = qstate["a"]
                quant_segA(i % 3, i % 2, qspecs[i][2])
                qstate["a"] += 1

    def flush_quant():
        while (qstate["b2"] < qstate["b1"] or qstate["b1"] < qstate["a"]
               or qstate["a"] < len(qspecs)):
            pop_quant(1)

    def gproj_panel(pan):
        wg = next_w()
        load_panel(wg, gw, 0, D, pan * 256, (pan + 1) * 256)
        use_r("tensor", sW[wg], sBIG)
        for sub in range(2):
            gt = pan * 2 + sub
            for ts in range(2):
                pp_ = next_ps()
                use_w("tensor", sPS[pp_])
                for k in range(ND):
                    mm(PS[pp_][:], wtile(wg, k, 256, sub * 128, P),
                       BIG[:, k * 1024 + ts * 512:k * 1024 + (ts + 1) * 512],
                       start=(k == 0), stop=(k == ND - 1),
                       inc="pe" if k == ND - 1 else None)
                mark_w("pe", sPS[pp_])
                use_r("vector", sPS[pp_], sBC[0])
                use_w("vector", sSCR[ts])
                em("vector", lambda E, ts=ts, pp_=pp_: E.tensor_tensor(
                    SCR[ts][:, 0:512], PS[pp_][:],
                    BC[0][:, ts * 512:(ts + 1) * 512], ALU.mult), inc="dv")
                mark_w("dv", sSCR[ts])
                mark_r("dv", sPS[pp_])
                use_r("scalar", sSCR[ts])
                use_w("scalar", sOS[ts])
                em("scalar", lambda E, ts=ts: E.activation(
                    OS[ts][:, 0:512], SCR[ts][:, 0:512], AF.Silu), inc="ac")
                mark_w("ac", sOS[ts])
                sSCR[ts].r = ("ac", cnt["ac"])
                dma("scalar", sg_scr[gt * P:(gt + 1) * P, ts * 512:(ts + 1) * 512],
                    OS[ts][:, 0:512], "da", oslot=sDR["sg"], islots=(sOS[ts],))
        sW[wg].r = ("pe", cnt["pe"])
        if pan % 2 == 0:
            pop_quant(1)


    for gp2 in range(NH // 2):
        for lane in range(2):
            h = 2 * gp2 + lane
            qdst = QG if lane == 0 else None
            for which, wdram in ((0, qw), (1, kw)):
                wsl_ = next_w()
                load_panel(wsl_, wdram, 0, D, 256 * h, 256 * (h + 1))
                use_r("tensor", sW[wsl_], sBIG)
                if lane == 0:
                    dst = (QG, KG)[which]
                    dsts = (sQG, sKG)[which]
                    dof = 0
                else:
                    dst = OSC
                    dsts = sOSC
                    dof = which * 2048
                scl = (1.0 / 16.0) if which == 0 else 1.0
                for ts in range(2):
                    pa, pb = next_ps(), next_ps()
                    use_w("tensor", sPS[pa], sPS[pb])
                    for half in range(2):
                        pt = (pa, pb)[half]
                        for k in range(ND):
                            mm(PS[pt][:], wtile(wsl_, k, 256, half * 128, P),
                               BIG[:, k * 1024 + ts * 512:k * 1024 + (ts + 1) * 512],
                               start=(k == 0), stop=(k == ND - 1),
                               inc="pe" if (k == ND - 1 and half == 1) else None)
                    mark_w("pe", sPS[pa], sPS[pb])
                    use_r("vector", sPS[pa], sPS[pb], sTAB)
                    sc0, sc1 = sSCR[ts % 2], sSCR[2 + ts % 2]
                    c0_ = SCR[ts % 2][:, 0:512]
                    c1_ = SCR[2 + ts % 2][:, 0:512]
                    use_w("vector", sc0, sc1, dsts)
                    cs = TCK[:, ts * 512:(ts + 1) * 512]
                    sn = TSK[:, ts * 512:(ts + 1) * 512]
                    d0 = dst[:, dof + ts * 512:dof + (ts + 1) * 512]
                    d1 = dst[:, dof + 1024 + ts * 512:dof + 1024 + (ts + 1) * 512]
                    em("vector", lambda E, cs=cs, a=c0_, pa=pa, s=scl:
                       E.scalar_tensor_tensor(a, PS[pa][:], s, cs, ALU.mult,
                                              ALU.mult))
                    em("vector", lambda E, sn=sn, b=c1_, pb=pb, s=scl:
                       E.scalar_tensor_tensor(b, PS[pb][:], s, sn, ALU.mult,
                                              ALU.mult))
                    em("vector", lambda E, d0=d0, a=c0_, b=c1_: E.tensor_sub(
                        d0, a, b))
                    em("vector", lambda E, cs=cs, a=c0_, pb=pb, s=scl:
                       E.scalar_tensor_tensor(a, PS[pb][:], s, cs, ALU.mult,
                                              ALU.mult))
                    em("vector", lambda E, sn=sn, b=c1_, pa=pa, s=scl:
                       E.scalar_tensor_tensor(b, PS[pa][:], s, sn, ALU.mult,
                                              ALU.mult))
                    em("vector", lambda E, d1=d1, a=c0_, b=c1_: E.tensor_add(
                        d1, a, b), inc="dv")
                    mark_w("dv", dsts, sc0, sc1)
                    mark_r("dv", sPS[pa], sPS[pb])
                sW[wsl_].r = ("pe", cnt["pe"])
            qsrc = QG if lane == 0 else OSC
            for i in range(2):
                dma("scalar", q_scr[(h * 2 + i) * P:(h * 2 + i + 1) * P, :],
                    qsrc[:, i * 1024:(i + 1) * 1024], "da", oslot=sDR["q"],
                    islots=(sQG if lane == 0 else sOSC,))
            va, vb = next_w(), next_w()
            load_panel(va, vw, 0, D, 512 * h, 512 * h + 256)
            load_panel(vb, vw, 0, D, 512 * h + 256, 512 * (h + 1))
            use_r("tensor", sW[va], sW[vb], sBIG, sSTC)
            for tt in range(NT):
                pv = next_ps()
                use_w("tensor", sPS[pv])
                for k in range(ND):
                    lt = BIG[:, k * 1024 + tt * 128:k * 1024 + (tt + 1) * 128]
                    mm(PS[pv][:, 0:256], lt, wtile(va, k, 256, 0, 256),
                       start=(k == 0), stop=False)
                    mm(PS[pv][:, 256:512], lt, wtile(vb, k, 256, 0, 256),
                       start=False, stop=(k == ND - 1),
                       inc="pe" if k == ND - 1 else None)
                mark_w("pe", sPS[pv])
                use_r("vector", sPS[pv], sSTC)
                use_w("vector", sBIG)
                dv_ = BIG[:, 16384 + lane * 4096 + tt * 512:
                          16384 + lane * 4096 + (tt + 1) * 512]
                em("vector", lambda E, d=dv_, tt=tt, pv=pv: E.tensor_scalar(
                    d, PS[pv][:], STC[:, 2 * tt:2 * tt + 1], None, ALU.mult),
                    inc="dv")
                mark_w("dv", sBIG)
                mark_r("dv", sPS[pv])
            sW[va].r = ("pe", cnt["pe"])
            sW[vb].r = ("pe", cnt["pe"])

        LA = dict(h=2 * gp2, qb=QG, qof=0, kb=KG, kof=0, sq=sQG, sk=sKG,
                  pa=0, po=2, vof=0, obs=(0, 1), ptof=0, ost=4,
                  SXt=SX, sSXt=sSX, SBt=SBX, sSBt=sSBX)
        LB = dict(h=2 * gp2 + 1, qb=OSC, qof=0, kb=OSC, kof=2048, sq=sOSC,
                  sk=sOSC, pa=1, po=3, vof=4096, obs=(2, 3), ptof=256, ost=5,
                  SXt=BC[1], sSXt=sSXB, SBt=SBXB, sSBt=sSBXB)

        def stage_front(L, n):
            h = L["h"]
            ob = OS[L["obs"][n % 2]]
            sob = sOS[L["obs"][n % 2]]
            pa = PS[L["pa"]]
            spa = sPS[L["pa"]]
            qcs = [L["qb"][:, L["qof"] + i * 1024 + n * 128:
                           L["qof"] + i * 1024 + (n + 1) * 128] for i in range(2)]
            kcs = [L["kb"][:, L["kof"] + i * 1024 + n * 128:
                           L["kof"] + i * 1024 + (n + 1) * 128] for i in range(2)]
            use_r("tensor", L["sq"], L["sk"])
            use_w("tensor", spa)
            for i in range(2):
                mm(pa[:, 0:128], kcs[i], qcs[i], start=(i == 0),
                   stop=(i == 1), inc="pe" if i == 1 else None)
            mark_w("pe", spa)
            use_w("tensor", sPTR)
            for i in range(2):
                em("tensor", lambda E, i=i, kk=kcs[i], o=L["ptof"]: E.transpose(
                    PTR[:, o + i * 128:o + (i + 1) * 128], kk, EYE[:]),
                    inc="pe" if i == 1 else None)
            mark_w("pe", sPTR)
            use_r("vector", spa, sPTR)
            use_w("vector", sob)
            em("vector", lambda E, ob=ob, pa=pa, h=h: E.tensor_tensor(
                ob[:, 0:128], pa[:, 0:128],
                DMK[:, h * 128:(h + 1) * 128], ALU.mult), inc="dv")
            mark_r("dv", spa)
            for i in range(2):
                em("vector", lambda E, ob=ob, i=i, qq=qcs[i], h=h:
                   E.tensor_tensor(
                       ob[:, 128 + i * 128:128 + (i + 1) * 128], qq,
                       CDT[:, h * 128:(h + 1) * 128], ALU.mult))
            em("vector", lambda E, ob=ob, h=h, o=L["ptof"]: E.tensor_scalar(
                ob[:, 384:640], PTR[:, o:o + 256], KDC[:, h:h + 1], None,
                ALU.mult), inc="dv")
            mark_w("dv", sob)
            sPTR.r = ("dv", cnt["dv"])

        def stage_back(L, n):
            h = L["h"]
            ob = OS[L["obs"][n % 2]]
            sob = sOS[L["obs"][n % 2]]
            po = PS[L["po"]]
            spo = sPS[L["po"]]
            SXt, sSXt, SBt, sSBt = L["SXt"], L["sSXt"], L["SBt"], L["sSBt"]
            vch = BIG[:, 16384 + L["vof"] + n * 512:
                      16384 + L["vof"] + (n + 1) * 512]
            use_r("tensor", sob, sBIG, sSBt)
            use_w("tensor", spo)
            for e in range(4):
                last = (n == 0 and e == 3)
                mm(po[:, e * 128:(e + 1) * 128],
                   vch[:, e * 128:(e + 1) * 128], ob[:, 0:128],
                   start=(e == 0), stop=last, inc="pe" if last else None)
            if n > 0:
                for e in range(4):
                    for i in range(2):
                        last = (e == 3 and i == 1)
                        mm(po[:, e * 128:(e + 1) * 128],
                           SBt[:, i * 512 + e * 128:i * 512 + (e + 1) * 128],
                           ob[:, 128 + i * 128:128 + (i + 1) * 128],
                           start=False, stop=last,
                           inc="pe" if last else None)
            mark_w("pe", spo)
            mark_r("pe", sSBt)
            use_w("tensor", sPS[4], sPS[5])
            for kt in range(2):
                mm(PS[4 + kt][:], ob[:, 384 + kt * 128:384 + (kt + 1) * 128],
                   vch, start=True, stop=True,
                   inc="pe" if kt == 1 else None)
            mark_w("pe", sPS[4], sPS[5])
            mark_r("pe", sob)
            ost, sost = OS[L["ost"]], sOS[L["ost"]]
            use_r("vector", spo)
            use_w("vector", sost)
            em("vector", lambda E, po=po, o=ost: E.tensor_copy(
                o[:, 0:512], po[:, 0:512]), inc="dv")
            mark_w("dv", sost)
            mark_r("dv", spo)
            oout = o_scr[(h * 4) * P:(h * 4 + 4) * P, n * 128:(n + 1) * 128] \
                .rearrange("(e p) c -> p e c", p=P)
            dma("scalar", oout, ost[:, 0:512], "da", oslot=sDR["o"],
                islots=(sost,))
            use_r("vector", sPS[4], sPS[5])
            use_w("vector", sSXt, sSBt)
            gch = float(GAMMA[h]) ** CH
            for kt in range(2):
                sx_sl = SXt[:, kt * 512:(kt + 1) * 512]
                sb_sl = SBt[:, kt * 512:(kt + 1) * 512]
                if n == 0:
                    em("vector", lambda E, s=sx_sl, kt=kt: E.tensor_copy(
                        s, PS[4 + kt][:]))
                else:
                    em("vector", lambda E, s=sx_sl, kt=kt, gch=gch:
                       E.scalar_tensor_tensor(s, s, gch, PS[4 + kt][:],
                                              ALU.mult, ALU.add))
                em("vector", lambda E, s=sx_sl, b=sb_sl: E.tensor_copy(b, s),
                   inc="dv" if kt == 1 else None)
            mark_w("dv", sSXt, sSBt)
            mark_r("dv", sPS[4], sPS[5])

        stage_front(LA, 0)
        stage_front(LB, 0)
        for n in range(1, NCH):
            stage_front(LA, n)
            stage_back(LA, n - 1)
            stage_front(LB, n)
            stage_back(LB, n - 1)
        stage_back(LA, NCH - 1)
        stage_back(LB, NCH - 1)
        for L in (LA, LB):
            for kt in range(2):
                dma("scalar",
                    sloc_scr[(L["h"] * 2 + kt) * P:(L["h"] * 2 + kt + 1) * P, :],
                    L["SBt"][:, kt * 512:(kt + 1) * 512], "da",
                    oslot=sDR["sloc"], islots=(L["sSBt"],))
        mark_r("pe", sQG, sKG, sBIG, sOSC)
    barrier()

    if sDR["sloc"].w is not None:
        wt("gpsimd", *sDR["sloc"].w)
    em("gpsimd", lambda E: E.collective_compute(
        "AllGather", ALU.bypass, replica_groups=[[0, 1, 2, 3], [4, 5, 6, 7]],
        ins=[sloc_scr.opt()], outs=[g1out.opt()]), inc="gp")
    cc1 = cnt["gp"]

    for pan in range(16):
        gproj_panel(pan)
    barrier()




    wt("sync", "gp", cc1)
    for h in range(NH):
        for kt in range(2):
            for j in range(3):
                dma("sync", OS[kt * 3 + j][:, 0:512],
                    g1out[j * 2048 + h * 256 + kt * 128:
                          j * 2048 + h * 256 + (kt + 1) * 128, :],
                    "da", oslot=sOS[kt * 3 + j])
        for kt in range(2):
            acc = SCR[kt][:, 0:512]
            sacc = sSCR[kt]
            for j in range(3):
                o_ = OS[kt * 3 + j]
                use_r("vector", sOS[kt * 3 + j])
                use_w("vector", sacc)
                if j == 0:
                    em("vector", lambda E, a=acc, j=j, h=h, o_=o_:
                       E.tensor_scalar(a, o_[:, 0:512],
                                       WCB[:, j * 8 + h:j * 8 + h + 1],
                                       None, ALU.mult), inc="dv")
                else:
                    em("vector", lambda E, a=acc, j=j, h=h, o_=o_:
                       E.scalar_tensor_tensor(
                           a, o_[:, 0:512], WCB[:, j * 8 + h:j * 8 + h + 1],
                           a, ALU.mult, ALU.add), inc="dv")
                mark_w("dv", sacc)
                sOS[kt * 3 + j].r = ("dv", cnt["dv"])
            use_w("vector", sSBX)
            em("vector", lambda E, kt=kt, a=acc: E.tensor_copy(
                SBX[:, kt * 512:(kt + 1) * 512], a), inc="dv")
            mark_w("dv", sSBX)
            sacc.r = ("dv", cnt["dv"])
        for i in range(2):
            dma("sync", QG[:, i * 1024:(i + 1) * 1024],
                q_scr[(h * 2 + i) * P:(h * 2 + i + 1) * P, :], "da",
                oslot=sQG, islots=(sDR["q"],))
        dma("sync", OS[0][:, :], gamtab_i[h * P:(h + 1) * P, :], "da",
            oslot=sOS[0])
        use_r("vector", sQG, sOS[0])
        use_w("vector", sQG)
        for i in range(2):
            em("vector", lambda E, i=i: E.tensor_tensor(
                QG[:, i * 1024:(i + 1) * 1024], QG[:, i * 1024:(i + 1) * 1024],
                OS[0][:], ALU.mult), inc="dv" if i == 1 else None)
        mark_w("dv", sQG)
        sOS[0].r = ("dv", cnt["dv"])
        for ts in range(2):
            cb = KG if ts == 0 else QG
            scb = sKG if ts == 0 else sQG
            sgslots = [(OSC, sOSC, e * 512) if ts == 0 else
                       (OS[e], sOS[e], 0) for e in range(4)]
            for e in range(4):
                dma("sync", OSC[:, (4 + e) * 512:(5 + e) * 512],
                    o_scr[(h * 4 + e) * P:(h * 4 + e + 1) * P,
                          ts * 512:(ts + 1) * 512], "da", oslot=sOSC,
                    islots=(sDR["o"],))
                sgb, ssgb, sgo = sgslots[e]
                dma("sync", sgb[:, sgo:sgo + 512],
                    sg_scr[(h * 4 + e) * P:(h * 4 + e + 1) * P,
                           ts * 512:(ts + 1) * 512], "da", oslot=ssgb,
                    islots=(sDR["sg"],))
            use_r("tensor", sSBX, sQG)
            use_w("tensor", sPS[0], sPS[1], sPS[2], sPS[3])
            for e in range(4):
                for i in range(2):
                    mm(PS[e][:],
                       SBX[:, i * 512 + e * 128:i * 512 + (e + 1) * 128],
                       QG[:, i * 1024 + ts * 512:i * 1024 + (ts + 1) * 512],
                       start=(i == 0), stop=(i == 1),
                       inc="pe" if (e == 3 and i == 1) else None)
            mark_w("pe", sPS[0], sPS[1], sPS[2], sPS[3])
            if ts == 1:
                mark_r("pe", sSBX, sQG)
            for e in range(4):
                use_r("vector", sOSC, sPS[e])
                use_w("vector", scb)
                em("vector", lambda E, e=e, cb=cb: E.tensor_add(
                    cb[:, e * 512:(e + 1) * 512],
                    OSC[:, (4 + e) * 512:(5 + e) * 512], PS[e][:]),
                    inc="dv")
                mark_w("dv", scb)
                mark_r("dv", sPS[e])
                use_r("scalar", scb)
                sqb = OS[4 + e % 2]
                use_w("scalar", sOS[4 + e % 2])
                em("scalar", lambda E, e=e, cb=cb, q=sqb: E.activation(
                    q[:, 0:512], cb[:, e * 512:(e + 1) * 512], AF.Square),
                    inc="ac")
                mark_w("ac", sOS[4 + e % 2])
                use_r("tensor", sOS[4 + e % 2])
                if e == 0:
                    use_w("tensor", sPRA if ts == 0 else sPRB)
                pr = PRA if ts == 0 else PRB
                mm(pr[0:1, :], ONC[:], sqb[:, 0:512], start=(e == 0),
                   stop=(e == 3), inc="pe")
                mark_r("pe", sOS[4 + e % 2])
            sOSC.r = ("dv", cnt["dv"])
            if ts == 0:
                mark_w("pe", sPRA)
            else:
                mark_w("pe", sPRB)
        use_r("vector", sPRA, sPRB)
        use_w("vector", sROW)
        em("vector", lambda E: E.tensor_scalar(
            ROW[0:1, 0:512], PRA[0:1, :], 1.0 / DV, 1e-6, ALU.mult, ALU.add))
        em("vector", lambda E: E.tensor_scalar(
            ROW[0:1, 512:1024], PRB[0:1, :], 1.0 / DV, 1e-6, ALU.mult,
            ALU.add), inc="dv")
        mark_w("dv", sROW)
        mark_r("dv", sPRA, sPRB)
        row_rstd(0, 1024, 1024)
        wt("vector", "ac")
        use_w("vector", sRWB)
        em("vector", lambda E: E.tensor_copy(RWB[0:1, 0:1024],
                                             ROW[0:1, 1024:2048]), inc="dv")
        mark_w("dv", sRWB)
        use_r("tensor", sRWB)
        use_w("tensor", sPS[4], sPS[5])
        mm(PS[4][:], ONR[:], RWB[0:1, 0:512], start=True, stop=True)
        mm(PS[5][:], ONR[:], RWB[0:1, 512:1024], start=True, stop=True,
           inc="pe")
        mark_w("pe", sPS[4], sPS[5])
        mark_r("pe", sRWB)
        for ts in range(2):
            cb = KG if ts == 0 else QG
            scb = sKG if ts == 0 else sQG
            pb = PS[4] if ts == 0 else PS[5]
            spb = sPS[4] if ts == 0 else sPS[5]
            for e in range(4):
                tb = OS[4 + e % 2]
                stb = sOS[4 + e % 2]
                use_r("vector", scb, spb)
                use_w("vector", stb)
                em("vector", lambda E, e=e, cb=cb, pb=pb, t=tb:
                   E.scalar_tensor_tensor(
                       t[:, 0:512], cb[:, e * 512:(e + 1) * 512],
                       GNW[:, e:e + 1], pb[:], ALU.mult, ALU.mult), inc="dv")
                mark_w("dv", stb)
                if e == 3:
                    mark_r("dv", spb, scb)
                sgb = OSC[:, e * 512:(e + 1) * 512] if ts == 0 \
                    else OS[e][:, 0:512]
                ssgb = sOSC if ts == 0 else sOS[e]
                use_r("vector", stb, ssgb)
                use_w("vector", sBIG)
                em("vector", lambda E, e=e, h=h, ts=ts, t=tb, s=sgb:
                   E.tensor_tensor(
                       BIG[:, (h * 4 + e) * 1024 + ts * 512:
                           (h * 4 + e) * 1024 + (ts + 1) * 512],
                       t[:, 0:512], s, ALU.mult), inc="dv")
                mark_w("dv", sBIG)
                stb.r = ("dv", cnt["dv"])
                ssgb.r = ("dv", cnt["dv"])
        pop_quant(1)


    pend = []
    for pan in range(8):
        ha, hb = next_w(), next_w()
        load_panel(ha, ow, 0, D, pan * 256, (pan + 1) * 256)
        load_panel(hb, ow, D, 2 * D, pan * 256, (pan + 1) * 256)
        use_r("tensor", sW[ha], sW[hb], sBIG)
        for sub in range(2):
            dt = pan * 2 + sub
            for ts in range(2):
                dma("sync", OS[ts][:, 0:512],
                    a_scr[dt * P:(dt + 1) * P, ts * 512:(ts + 1) * 512], "da",
                    oslot=sOS[ts], islots=(sDR["a"],))
                ph = next_ps()
                use_w("tensor", sPS[ph])
                for k in range(ND):
                    mm(PS[ph][:], wtile(ha, k, 256, sub * 128, P),
                       BIG[:, k * 1024 + ts * 512:k * 1024 + (ts + 1) * 512],
                       start=(k == 0), stop=False)
                for k in range(ND):
                    mm(PS[ph][:], wtile(hb, k, 256, sub * 128, P),
                       BIG[:, (16 + k) * 1024 + ts * 512:
                           (16 + k) * 1024 + (ts + 1) * 512],
                       start=False, stop=(k == ND - 1),
                       inc="pe" if k == ND - 1 else None)
                mark_w("pe", sPS[ph])
                use_r("vector", sPS[ph], sOS[ts])
                use_w("vector", sSCR[ts], sOS[2 + ts])
                em("vector", lambda E, ts=ts, ph=ph: E.tensor_add(
                    SCR[ts][:, 0:512], OS[ts][:, 0:512], PS[ph][:]))
                em("vector", lambda E, ts=ts, dt=dt: E.tensor_scalar(
                    OS[2 + ts][:, 0:512], SCR[ts][:, 0:512], MLW[:, dt:dt + 1],
                    None, ALU.mult), inc="dv")
                mark_w("dv", sSCR[ts], sOS[2 + ts])
                mark_r("dv", sPS[ph])
                sOS[ts].r = ("dv", cnt["dv"])
                dma("scalar", res2_scr[dt * P:(dt + 1) * P, ts * 512:(ts + 1) * 512],
                    SCR[ts][:, 0:512], "da", oslot=sDR["res2"],
                    islots=(sSCR[ts],))
                dma("scalar", u3_scr[dt * P:(dt + 1) * P, ts * 512:(ts + 1) * 512],
                    OS[2 + ts][:, 0:512], "da", oslot=sDR["u3"],
                    islots=(sOS[2 + ts],))
                kq = (2 * (dt % 2) + ts)
                use_r("scalar", sSCR[ts])
                use_w("scalar", sKG)
                em("scalar", lambda E, ts=ts, kq=kq: E.activation(
                    KG[:, kq * 512:(kq + 1) * 512], SCR[ts][:, 0:512],
                    AF.Square), inc="ac")
                mark_w("ac", sKG)
                sSCR[ts].r = ("ac", cnt["ac"])

                def _stats3(dt=dt, ts=ts, kq=kq):
                    use_r("tensor", sKG)
                    if dt == 0 and ts == 0:
                        use_w("tensor", sPRA, sPRB)
                    pr = PRA if ts == 0 else PRB
                    mm(pr[0:1, :], ONC[:], KG[:, kq * 512:(kq + 1) * 512],
                       start=(dt == 0), stop=(dt == ND - 1), inc="pe")
                    mark_r("pe", sKG)
                pend.append(_stats3)
                if len(pend) > 2:
                    pend.pop(0)()
                pop_quant(1)
        sW[ha].r = ("pe", cnt["pe"])
        sW[hb].r = ("pe", cnt["pe"])
    while pend:
        pend.pop(0)()
    flush_quant()
    if sDR["wqgu"].w is not None:
        wt("gpsimd", *sDR["wqgu"].w)
    em("gpsimd", lambda E: E.collective_compute(
        "AllGather", ALU.bypass, replica_groups=[list(range(NCORE))],
        ins=[wqgu_st.opt()], outs=[g2gu.opt()]), inc="gp")
    mark_w("pe", sPRA, sPRB)
    use_r("vector", sPRA, sPRB)
    use_w("vector", sROW)
    for ts in range(2):
        em("vector", lambda E, ts=ts, pr=(PRA, PRB)[ts]: E.tensor_scalar(
            ROW[0:1, ts * 512:(ts + 1) * 512], pr[0:1, :], 1.0 / D, 1e-6,
            ALU.mult, ALU.add), inc="dv" if ts == 1 else None)
    mark_w("dv", sROW)
    mark_r("dv", sPRA, sPRB)
    row_rstd(0, 1024, 1024)




    def mlp_half(H):
        pend2 = []
        for t in range(ND):
            dma("sync", BIG[:, 22528 + t * 512:22528 + (t + 1) * 512],
                u3_scr[t * P:(t + 1) * P, H * 512:(H + 1) * 512], "da",
                oslot=sBIG, islots=(sDR["u3"],))
        expand_row(ROW[0:1, 1024 + H * 512:1024 + (H + 1) * 512], 512,
                   BC[1][:, 0:512], sBC[1], 4)
        for pp in range(22):
            wg_, wy_ = next_w(), next_w()
            load_panel(wg_, gatew, 0, D, pp * 256, (pp + 1) * 256)
            load_panel(wy_, gatew, 0, D, INTER + pp * 256, INTER + (pp + 1) * 256)
            use_r("tensor", sW[wg_], sW[wy_], sBIG)
            for sub in range(2):
                st = pp * 2 + sub
                pg_, py_ = next_ps(), next_ps()
                for which, ww, pb in ((0, wg_, pg_), (1, wy_, py_)):
                    use_w("tensor", sPS[pb])
                    for k in range(ND):
                        mm(PS[pb][:], wtile(ww, k, 256, sub * 128, P),
                           BIG[:, 22528 + k * 512:22528 + (k + 1) * 512],
                           start=(k == 0), stop=(k == ND - 1),
                           inc="pe" if k == ND - 1 else None)
                    mark_w("pe", sPS[pb])
                use_r("vector", sPS[pg_], sPS[py_], sBC[1])
                use_w("vector", sSCR[0], sSCR[1])
                em("vector", lambda E, pg_=pg_: E.tensor_tensor(
                    SCR[0][:, 0:512], PS[pg_][:], BC[1][:, 0:512], ALU.mult))
                em("vector", lambda E, py_=py_: E.tensor_tensor(
                    SCR[1][:, 0:512], PS[py_][:], BC[1][:, 0:512], ALU.mult),
                    inc="dv")
                mark_w("dv", sSCR[0], sSCR[1])
                mark_r("dv", sPS[pg_], sPS[py_])
                use_r("scalar", sSCR[0])
                use_w("scalar", sOS[0])
                em("scalar", lambda E: E.activation(OS[0][:, 0:512],
                                                    SCR[0][:, 0:512], AF.Silu),
                   inc="ac")
                mark_w("ac", sOS[0])
                sSCR[0].r = ("ac", cnt["ac"])
                use_r("vector", sOS[0], sSCR[1])
                use_w("vector", sBIG)
                em("vector", lambda E, st=st: E.tensor_tensor(
                    BIG[:, st * 512:(st + 1) * 512], OS[0][:, 0:512],
                    SCR[1][:, 0:512], ALU.mult), inc="dv")
                mark_w("dv", sBIG)
                sOS[0].r = ("dv", cnt["dv"])
                sSCR[1].r = ("dv", cnt["dv"])
            sW[wg_].r = ("pe", cnt["pe"])
            sW[wy_].r = ("pe", cnt["pe"])
        for dpan in range(8):
            wa_, wb_, wc_ = next_w(), next_w(), next_w()
            load_panel(wa_, downw, 0, D, dpan * 256, (dpan + 1) * 256)
            load_panel(wb_, downw, D, 2 * D, dpan * 256, (dpan + 1) * 256)
            load_panel(wc_, downw, 2 * D, INTER, dpan * 256, (dpan + 1) * 256)
            use_r("tensor", sW[wa_], sW[wb_], sW[wc_], sBIG)
            for sub in range(2):
                dt = dpan * 2 + sub
                dma("sync", SCR[2][:, 0:512],
                    res2_scr[dt * P:(dt + 1) * P, H * 512:(H + 1) * 512], "da",
                    oslot=sSCR[2], islots=(sDR["res2"],))
                dma("sync", SCR[3][:, 0:512],
                    x0fm[dt * P:(dt + 1) * P, H * 512:(H + 1) * 512], "da",
                    oslot=sSCR[3])
                ph = next_ps()
                use_w("tensor", sPS[ph])
                for k in range(ND):
                    mm(PS[ph][:], wtile(wa_, k, 256, sub * 128, P),
                       BIG[:, k * 512:(k + 1) * 512], start=(k == 0),
                       stop=False)
                for k in range(ND):
                    mm(PS[ph][:], wtile(wb_, k, 256, sub * 128, P),
                       BIG[:, (16 + k) * 512:(16 + k + 1) * 512],
                       start=False, stop=False)
                for k in range(12):
                    mm(PS[ph][:], wtile(wc_, k, 256, sub * 128, P),
                       BIG[:, (32 + k) * 512:(32 + k + 1) * 512],
                       start=False, stop=(k == 11),
                       inc="pe" if k == 11 else None)
                mark_w("pe", sPS[ph])
                use_r("vector", sPS[ph], sSCR[2], sSCR[3])
                use_w("vector", sSCR[4])
                em("vector", lambda E, ph=ph: E.tensor_add(
                    SCR[4][:, 0:512], SCR[2][:, 0:512], PS[ph][:]))
                em("vector", lambda E: E.tensor_add(
                    SCR[4][:, 0:512], SCR[4][:, 0:512], SCR[3][:, 0:512]),
                    inc="dv")
                mark_w("dv", sSCR[4])
                mark_r("dv", sPS[ph])
                sSCR[2].r = ("dv", cnt["dv"])
                sSCR[3].r = ("dv", cnt["dv"])
                dma("scalar", x1_scr[dt * P:(dt + 1) * P, H * 512:(H + 1) * 512],
                    SCR[4][:, 0:512], "da", oslot=sDR["x1"], islots=(sSCR[4],))
                xb_, sxb_ = OS[2 + 2 * (dt % 2)], sOS[2 + 2 * (dt % 2)]
                qb_, sqb_ = OS[3 + 2 * (dt % 2)], sOS[3 + 2 * (dt % 2)]
                use_r("scalar", sSCR[4])
                use_w("scalar", sxb_, sqb_)
                em("scalar", lambda E, x=xb_: E.copy(x[:, 0:512],
                                                     SCR[4][:, 0:512]))
                em("scalar", lambda E, x=xb_, q=qb_: E.activation(
                    q[:, 0:512], x[:, 0:512], AF.Square), inc="ac")
                mark_w("ac", sxb_, sqb_)

                def _stats2(dt=dt, x=xb_, q=qb_, sx=sxb_, sq=sqb_):
                    use_r("tensor", sx, sq)
                    if dt == 0:
                        use_w("tensor", sPRA, sPRB)
                    mm(PRA[0:1, :], ONC[:], x[:, 0:512], start=(dt == 0),
                       stop=(dt == ND - 1))
                    mm(PRB[0:1, :], ONC[:], q[:, 0:512], start=(dt == 0),
                       stop=(dt == ND - 1), inc="pe")
                    mark_r("pe", sx, sq)
                pend2.append(_stats2)
                if len(pend2) > 1:
                    pend2.pop(0)()
            sW[wa_].r = ("pe", cnt["pe"])
            sW[wb_].r = ("pe", cnt["pe"])
            sW[wc_].r = ("pe", cnt["pe"])
        while pend2:
            pend2.pop(0)()
        mark_w("pe", sPRA, sPRB)
        use_r("vector", sPRA, sPRB)
        use_w("vector", sROW)
        em("vector", lambda E, H=H: E.tensor_scalar(
            ROW[0:1, 2048 + H * 512:2048 + (H + 1) * 512], PRA[0:1, :],
            1.0 / D, None, ALU.mult))
        em("vector", lambda E: E.tensor_scalar(
            ROW[0:1, 0:512], PRB[0:1, :], 1.0 / D, None, ALU.mult))
        em("vector", lambda E, H=H: E.tensor_tensor(
            ROW[0:1, 512:1024], ROW[0:1, 2048 + H * 512:2048 + (H + 1) * 512],
            ROW[0:1, 2048 + H * 512:2048 + (H + 1) * 512], ALU.mult))
        em("vector", lambda E: E.tensor_sub(
            ROW[0:1, 0:512], ROW[0:1, 0:512], ROW[0:1, 512:1024]))
        em("vector", lambda E: E.tensor_scalar(
            ROW[0:1, 0:512], ROW[0:1, 0:512], 1e-5, None, ALU.add), inc="dv")
        mark_w("dv", sROW)
        mark_r("dv", sPRA, sPRB)
        row_rstd(0, 3072 + H * 512)

    mlp_half(0)
    if sDR["wqp"].w is not None:
        wt("gpsimd", *sDR["wqp"].w)
    em("gpsimd", lambda E: E.collective_compute(
        "AllGather", ALU.bypass, replica_groups=[list(range(NCORE))],
        ins=[wqp_st.opt()], outs=[g2p.opt()]), inc="gp")
    cc2 = cnt["gp"]
    barrier()
    mlp_half(1)
    barrier()

    def b2tile(t):
        if t < 8:
            return OSC[:, t * 512:(t + 1) * 512]
        if t < 12:
            return QG[:, (t - 8) * 512:(t - 7) * 512]
        return KG[:, (t - 12) * 512:(t - 11) * 512]

    wt("sync", "gp", cc2)
    wt("gpsimd", "gp", cc2)
    for H in range(2):
        pend = []
        expand_row(ROW[0:1, 2048 + H * 512:2048 + (H + 1) * 512], 512,
                   BC[1][:, 0:512], sBC[1], 4)
        expand_row(ROW[0:1, 3072 + H * 512:3072 + (H + 1) * 512], 512,
                   BC[2][:, 0:512], sBC[2], 5)
        for t in range(ND):
            dma("sync", SCR[t % 2][:, 0:512],
                x1_scr[t * P:(t + 1) * P, H * 512:(H + 1) * 512], "da",
                oslot=sSCR[t % 2], islots=(sDR["x1"],))
            use_r("vector", sSCR[t % 2], sBC[1])
            b2s = sOSC if t < 8 else (sQG if t < 12 else sKG)
            use_w("vector", b2s)
            em("vector", lambda E, t=t: E.tensor_sub(
                b2tile(t), SCR[t % 2][:, 0:512], BC[1][:, 0:512]), inc="dv")
            mark_w("dv", b2s)
            sSCR[t % 2].r = ("dv", cnt["dv"])
        for pp in range(32):
            bg = pp // 8
            c0 = (pp * 256) % 2048
            wg_, wu_ = next_w(), next_w()
            load_panel(wg_, g2gu, bg * D, (bg + 1) * D, c0, c0 + 256, "sync")
            load_panel(wu_, g2gu, (4 + bg) * D, (5 + bg) * D, c0, c0 + 256,
                       "sync")
            use_r("tensor", sW[wg_], sW[wu_], sQG, sKG, sOSC)
            for sub in range(2):
                ht = pp * 2 + sub
                pg_, pu_ = next_ps(), next_ps()
                for ww, pb in ((wg_, pg_), (wu_, pu_)):
                    use_w("tensor", sPS[pb])
                    for k in range(ND):
                        mm(PS[pb][:], wtile(ww, k, 256, sub * 128, P),
                           b2tile(k), start=(k == 0), stop=(k == ND - 1),
                           inc="pe" if k == ND - 1 else None)
                    mark_w("pe", sPS[pb])
                use_r("vector", sPS[pg_], sPS[pu_], sBC[2])
                use_w("vector", sSCR[2], sSCR[3])
                em("vector", lambda E, pg_=pg_: E.tensor_tensor(
                    SCR[2][:, 0:512], PS[pg_][:], BC[2][:, 0:512], ALU.mult))
                em("vector", lambda E, pu_=pu_: E.tensor_tensor(
                    SCR[3][:, 0:512], PS[pu_][:], BC[2][:, 0:512], ALU.mult),
                    inc="dv")
                mark_w("dv", sSCR[2], sSCR[3])
                mark_r("dv", sPS[pg_], sPS[pu_])
                use_r("scalar", sSCR[2])
                use_w("scalar", sOS[0])
                em("scalar", lambda E: E.activation(
                    OS[0][:, 0:512], SCR[2][:, 0:512], AF.Silu), inc="ac")
                mark_w("ac", sOS[0])
                sSCR[2].r = ("ac", cnt["ac"])
                use_r("vector", sOS[0], sSCR[3])
                use_w("vector", sBIG)
                em("vector", lambda E, ht=ht: E.tensor_tensor(
                    BIG[:, ht * 512:(ht + 1) * 512], OS[0][:, 0:512],
                    SCR[3][:, 0:512], ALU.mult), inc="dv")
                mark_w("dv", sBIG)
                sOS[0].r = ("dv", cnt["dv"])
                sSCR[3].r = ("dv", cnt["dv"])
                sqb, ssqb = OS[2 + ht % 2], sOS[2 + ht % 2]
                use_r("scalar", sBIG)
                use_w("scalar", ssqb)
                em("scalar", lambda E, ht=ht, q=sqb: E.activation(
                    q[:, 0:512], BIG[:, ht * 512:(ht + 1) * 512],
                    AF.Square), inc="ac")
                mark_w("ac", ssqb)

                def _stats(ht=ht, q=sqb, sq=ssqb):
                    use_r("tensor", sq)
                    if ht == 0:
                        use_w("tensor", sPRA)
                    mm(PRA[0:1, :], ONC[:], q[:, 0:512], start=(ht == 0),
                       stop=(ht == NHF - 1), inc="pe")
                    mark_r("pe", sq)
                pend.append(_stats)
                if len(pend) > 1:
                    pend.pop(0)()
            sW[wg_].r = ("pe", cnt["pe"])
            sW[wu_].r = ("pe", cnt["pe"])
        while pend:
            pend.pop(0)()
        mark_w("pe", sPRA)
        use_r("vector", sPRA)
        use_w("vector", sROW)
        em("vector", lambda E: E.tensor_scalar(
            ROW[0:1, 0:512], PRA[0:1, :], 1.0 / HF, 1.1920929e-7, ALU.mult,
            ALU.add), inc="dv")
        mark_w("dv", sROW)
        mark_r("dv", sPRA)
        row_rstd(0, 512)
        expand_row(ROW[0:1, 512:1024], 512, BC[1][:, 0:512], sBC[1], 4)
        for b in range(8):
            wsl4 = [next_w() for _ in range(4)]
            for r4 in range(4):
                load_panel(wsl4[r4], g2p, b * HF + r4 * D, b * HF + (r4 + 1) * D,
                           0, 256, "sync")
            use_r("tensor", *[sW[i] for i in wsl4], sBIG)
            for dsub in range(2):
                dt = b * 2 + dsub
                ph = next_ps()
                use_w("tensor", sPS[ph])
                for r4 in range(4):
                    for k in range(ND):
                        mm(PS[ph][:], wtile(wsl4[r4], k, 256, dsub * 128, P),
                           BIG[:, (r4 * 16 + k) * 512:(r4 * 16 + k + 1) * 512],
                           start=(r4 == 0 and k == 0),
                           stop=(r4 == 3 and k == ND - 1),
                           inc="pe" if (r4 == 3 and k == ND - 1) else None)
                mark_w("pe", sPS[ph])
                dma("sync", SCR[2][:, 0:512],
                    x1_scr[dt * P:(dt + 1) * P, H * 512:(H + 1) * 512], "da",
                    oslot=sSCR[2], islots=(sDR["x1"],))
                use_r("vector", sPS[ph], sSCR[2], sBC[1])
                use_w("vector", sSCR[4])
                em("vector", lambda E, ph=ph: E.tensor_tensor(
                    SCR[4][:, 0:512], PS[ph][:], BC[1][:, 0:512], ALU.mult))
                em("vector", lambda E: E.tensor_add(
                    SCR[4][:, 0:512], SCR[4][:, 0:512], SCR[2][:, 0:512]),
                    inc="dv")
                mark_w("dv", sSCR[4])
                mark_r("dv", sPS[ph])
                sSCR[2].r = ("dv", cnt["dv"])
                dma("sync", y_out[dt * P:(dt + 1) * P, H * 512:(H + 1) * 512],
                    SCR[4][:, 0:512], "out", islots=(sSCR[4],))
            for i in wsl4:
                sW[i].r = ("pe", cnt["pe"])
        barrier()

    if debug:
        sloc_dbg = dram("sloc_dbg", [2 * NH * P, DV], BF16, "ExternalOutput").ap()
        wqgu_dbg = dram("wqgu_dbg", [D, D], BF16, "ExternalOutput").ap()
        dma("sync", sloc_dbg, sloc_scr, "out", islots=(sDR["sloc"],))
        dma("sync", wqgu_dbg, wqgu_st, "out", islots=(sDR["wqgu"],))

    wt("sync", "out", cnt["out"])

    from contextlib import ExitStack as ES2
    with es:
        with ES2() as es2:
            sem_handles = {n: es2.enter_context(nc.semaphore(f"sem_{n}"))
                           for n in cnt.keys()}
            with nc.Block() as block:
                def mk(engname):
                    def body(eng):
                        for it in Q[engname]:
                            if it[0] == "wait":
                                eng.wait_ge(sem_handles[it[1]], it[2])
                            else:
                                inst = it[1](eng)
                                if it[2] is not None:
                                    inst.then_inc(sem_handles[it[2]], it[3])
                    body.__name__ = engname
                    return body
                block.sync(mk("sync"))
                block.scalar(mk("scalar"))
                block.vector(mk("vector"))
                block.tensor(mk("tensor"))
                block.gpsimd(mk("gpsimd"))
    return nc


def _host_tables():
    i_idx = np.arange(CH, dtype=np.float32)
    p_idx = np.arange(T, dtype=np.float32)
    dmaskT = np.zeros((P, NH * CH), dtype=bfnp)
    cdecT = np.zeros((P, NH * CH), dtype=bfnp)
    kdec = np.zeros((P, NH), dtype=np.float32)
    gamtab = np.zeros((NH * P, T), dtype=bfnp)
    for h in range(NH):
        g = np.float32(GAMMA[h])
        jj, ii = i_idx[:, None], i_idx[None, :]
        m = np.where(ii >= jj, g ** (ii - jj), 0.0).astype(np.float32)
        dmaskT[:, h * CH:(h + 1) * CH] = m.astype(bfnp)
        cd = (g ** (i_idx + 1.0)).astype(np.float32)
        cdecT[:, h * CH:(h + 1) * CH] = \
            np.broadcast_to(cd[None, :], (P, CH)).astype(bfnp)
        kdec[:, h] = (g ** (CH - 1.0 - i_idx)).astype(np.float32)
        gt = (g ** (p_idx + 1.0)).astype(np.float32)
        gamtab[h * P:(h + 1) * P, :] = \
            np.broadcast_to(gt[None, :], (P, T)).astype(bfnp)
    return dmaskT, cdecT, kdec, gamtab


def kernel(L0_attn_hidden_states, L0_attn_attn_norm_w, L0_attn_q_w, L0_attn_k_w,
           L0_attn_v_w, L0_attn_g_w, L0_attn_o_w, L0_attn_gnorm_w,
           L0_attn_mlp_norm_w, L0_attn_gate_w, L0_attn_down_w,
           L0_ffn_w_gate_up, L0_ffn_w_proj):
    x = np.asarray(L0_attn_hidden_states, np.float32)
    qw = np.ascontiguousarray(np.asarray(L0_attn_q_w, np.float32))
    kw = np.ascontiguousarray(np.asarray(L0_attn_k_w, np.float32))
    vw = np.ascontiguousarray(np.asarray(L0_attn_v_w, np.float32))
    gw = np.ascontiguousarray(np.asarray(L0_attn_g_w, np.float32))
    ow = np.ascontiguousarray(np.asarray(L0_attn_o_w, np.float32))
    gatew = np.ascontiguousarray(np.asarray(L0_attn_gate_w, np.float32))
    downw = np.ascontiguousarray(np.asarray(L0_attn_down_w, np.float32))
    wguT = np.ascontiguousarray(np.asarray(L0_ffn_w_gate_up, np.float32).T)
    wpT = np.ascontiguousarray(np.asarray(L0_ffn_w_proj, np.float32).T)
    attw = np.ascontiguousarray(
        np.asarray(L0_attn_attn_norm_w, np.float32).reshape(ND, P).T)
    mlpw = np.ascontiguousarray(
        np.asarray(L0_attn_mlp_norm_w, np.float32).reshape(ND, P).T)
    gnw = np.ascontiguousarray(
        np.asarray(L0_attn_gnorm_w, np.float32).reshape(4, P).T)

    dmaskT, cdecT, kdec, gamtab = _host_tables()
    eye = np.eye(P, dtype=bfnp)
    onesb = np.zeros((P, 2), dtype=bfnp)
    onesb[0:64, 0] = 1
    onesb[64:128, 1] = 1
    onesc = np.ones((P, 1), dtype=bfnp)
    onesr = np.ones((1, P), dtype=bfnp)
    inv_freq = (1.0 / (10000.0 ** (np.arange(0, 256, 2, dtype=np.float32)
                                   / np.float32(256)))).astype(np.float32)

    xt = np.ascontiguousarray(x.transpose(0, 2, 1))

    nc = build_program(debug=DEBUG_BUILD)
    in_maps = []
    for c in range(NCORE):
        b, qtr = c // 4, c % 4
        pos = np.arange(qtr * T, (qtr + 1) * T, dtype=np.float32)
        ang = pos[None, :] * inv_freq[:, None]
        wcomb = np.zeros((P, 64), dtype=np.float32)
        for j in range(qtr):
            for h in range(NH):
                wcomb[:, j * 8 + h] = \
                    np.float32(GAMMA[h]) ** np.float32(T * (qtr - 1 - j))
        in_maps.append(dict(
            x0fm=np.ascontiguousarray(xt[b, :, qtr * T:(qtr + 1) * T]),
            qw=qw, kw=kw, vw=vw, gw=gw, ow=ow, gatew=gatew, downw=downw,
            wgu_sh=np.ascontiguousarray(wguT[:, c * D:(c + 1) * D]),
            wp_sh=np.ascontiguousarray(wpT[:, c * 256:(c + 1) * 256]),
            cosraw=np.cos(ang).astype(np.float32),
            sinraw=np.sin(ang).astype(np.float32),
            attw_i=attw, mlpw_i=mlpw, gnw_i=gnw,
            dmaskT_i=dmaskT, cdecT_i=cdecT, kdec_i=kdec, gamtab_i=gamtab,
            wcomb_i=wcomb, eye_i=eye, onesb_i=onesb, onesc_i=onesc,
            onesr_i=onesr,
        ))
    res = run_bass_kernel_spmd(nc, in_maps, core_ids=list(range(NCORE)))
    global _last_results
    _last_results = res.results
    out = np.empty((2, 4096, D), np.float32)
    for c in range(NCORE):
        b, qtr = c // 4, c % 4
        out[b, qtr * T:(qtr + 1) * T, :] = np.asarray(res.results[c]["y"]).T
    return out


if __name__ == "__main__":
    import reference as R
    ins = {k: np.asarray(v) for k, v in R.setup_inputs(0).items()}
    got = kernel(**ins)
    ref = np.asarray(R.reference(**ins), np.float32)
    err = got - ref
    rvr = float(np.mean(err ** 2) / np.mean(ref.astype(np.float64) ** 2))
    print(f"residual-variance ratio = {rvr:.3e} (bar 1.40e-04)")
    bd = np.abs(got.astype(bfnp).astype(np.float32)
                - ref.astype(bfnp).astype(np.float32)).max()
    print(f"max bf16 diff = {bd:.4f} (bar 0.422)")


# version 23 (final)
# speedup vs baseline: 1.0086x; 1.0086x over previous
import numpy as np
import ml_dtypes
import concourse.bass as bass
import concourse.mybir as mybir
from concourse.bass_utils import run_bass_kernel_spmd

F32 = mybir.dt.float32
BF16 = mybir.dt.bfloat16
AF = mybir.ActivationFunctionType
ALU = mybir.AluOpType
bfnp = ml_dtypes.bfloat16

NCORE = 8
P = 128
T = 1024
NT = 8
D = 2048
ND = 16
NH = 8
DK = 256
DV = 512
CH = 128
NCH = T // CH
HPG = 1
NG = NH // HPG
GAMMA = [float(1.0 - 2.0 ** (-5 - h)) for h in range(NH)]
MAGIC = 12582912.0
INTER = 5632
NSU = INTER // P
HF = 8192
NHF = HF // P
ENGS = ("sync", "scalar", "vector", "tensor", "gpsimd")
DMA_SEMS = ("dw", "da", "out")
DEBUG_BUILD = False
_last_results = None


_slot_n = [0]


class Slot:
    __slots__ = ("w", "r", "ds")

    def __init__(self):
        self.w = None
        self.r = None
        self.ds = f"dma{_slot_n[0]}"
        _slot_n[0] += 1


def build_program(debug=False):
    nc = bass.Bass("TRN2", target_bir_lowering=False, debug=False)
    scratch_kind = "ExternalOutput" if debug else None

    def dram(name, shape, dtype, kind=None):
        if kind is None:
            return nc.dram_tensor(name, shape, dtype)
        return nc.dram_tensor(name, shape, dtype, kind=kind)

    x0fm = dram("x0fm", [D, T], F32, "ExternalInput").ap()
    qw = dram("qw", [D, D], F32, "ExternalInput").ap()
    kw = dram("kw", [D, D], F32, "ExternalInput").ap()
    vw = dram("vw", [D, 2 * D], F32, "ExternalInput").ap()
    gw = dram("gw", [D, 2 * D], F32, "ExternalInput").ap()
    ow = dram("ow", [2 * D, D], F32, "ExternalInput").ap()
    gatew = dram("gatew", [D, 2 * INTER], F32, "ExternalInput").ap()
    downw = dram("downw", [INTER, D], F32, "ExternalInput").ap()
    wgu_sh = dram("wgu_sh", [D, D], F32, "ExternalInput").ap()
    wp_sh = dram("wp_sh", [HF, 256], F32, "ExternalInput").ap()
    cosraw = dram("cosraw", [P, T], F32, "ExternalInput").ap()
    sinraw = dram("sinraw", [P, T], F32, "ExternalInput").ap()
    attw_i = dram("attw_i", [P, ND], F32, "ExternalInput").ap()
    mlpw_i = dram("mlpw_i", [P, ND], F32, "ExternalInput").ap()
    gnw_i = dram("gnw_i", [P, 4], F32, "ExternalInput").ap()
    dmaskT_i = dram("dmaskT_i", [P, NH * CH], BF16, "ExternalInput").ap()
    cdecT_i = dram("cdecT_i", [P, NH * CH], BF16, "ExternalInput").ap()
    kdec_i = dram("kdec_i", [P, NH], F32, "ExternalInput").ap()
    gamtab_i = dram("gamtab_i", [NH * P, T], BF16, "ExternalInput").ap()
    wcomb_i = dram("wcomb_i", [P, 64], F32, "ExternalInput").ap()
    eye_i = dram("eye_i", [P, P], BF16, "ExternalInput").ap()
    onesb_i = dram("onesb_i", [P, 2], BF16, "ExternalInput").ap()
    onesc_i = dram("onesc_i", [P, 1], BF16, "ExternalInput").ap()
    onesr_i = dram("onesr_i", [1, P], BF16, "ExternalInput").ap()

    y_out = dram("y", [D, T], F32, "ExternalOutput").ap()

    a_scr = dram("a_scr", [ND * P, T], BF16, scratch_kind).ap()
    q_scr = dram("q_scr", [ND * P, T], BF16, scratch_kind).ap()
    o_scr = dram("o_scr", [32 * P, T], BF16, scratch_kind).ap()
    sg_scr = dram("sg_scr", [32 * P, T], BF16, scratch_kind).ap()
    res2_scr = dram("res2_scr", [ND * P, T], F32, scratch_kind).ap()
    u3_scr = dram("u3_scr", [ND * P, T], BF16, scratch_kind).ap()
    x1_scr = dram("x1_scr", [ND * P, T], F32, scratch_kind).ap()
    sloc_scr = dram("sloc_scr", [2 * NH * P, DV], BF16).ap()
    g1out = dram("g1out", [4 * 2 * NH * P, DV], BF16).ap()
    wqgu_st = dram("wqgu_st", [D, D], BF16).ap()
    wqp_st = dram("wqp_st", [HF, 256], BF16).ap()
    g2gu = nc.dram_tensor("g2gu", [NCORE * D, D], BF16, addr_space="Shared").ap()
    g2p = nc.dram_tensor("g2p", [NCORE * HF, 256], BF16, addr_space="Shared").ap()
    rowdr = dram("rowdr", [8, T], F32).ap()
    rowbf = dram("rowbf", [4, 2048], BF16).ap()

    from collections import defaultdict
    Q = {e: [] for e in ENGS}
    cnt = defaultdict(int)
    lastw = {e: {} for e in ENGS}

    def is_dma_sem(s):
        return s.startswith("dma") or s in ("out", "dtab")

    def em(eng, fn, inc=None):
        step = None
        if inc is not None:
            step = 16 if is_dma_sem(inc) else 1
            cnt[inc] += step
        Q[eng].append(("op", fn, inc, step))
        return cnt[inc] if inc else None

    def wt(eng, sem, val=None):
        v = cnt[sem] if val is None else val
        if v is None or v <= 0 or v <= lastw[eng].get(sem, 0):
            return
        lastw[eng][sem] = v
        Q[eng].append(("wait", sem, v))

    def use_w(eng, *slots):
        for s in slots:
            if s.r is not None:
                wt(eng, *s.r)
            if s.w is not None:
                wt(eng, *s.w)

    def use_r(eng, *slots):
        for s in slots:
            if s.w is not None:
                wt(eng, *s.w)

    def mark_w(sem, *slots):
        for s in slots:
            s.w = (sem, cnt[sem])

    def mark_r(sem, *slots):
        for s in slots:
            s.r = (sem, cnt[sem])

    def barrier():
        for e in ENGS:
            em(e, lambda E: E.nop(), inc="b_" + e)
        vals = {e: cnt["b_" + e] for e in ENGS}
        for e in ENGS:
            for o in ENGS:
                if o != e:
                    wt(e, "b_" + o, vals[o])

    from contextlib import ExitStack
    es = ExitStack()

    def sb(name, shape, dtype):
        return es.enter_context(nc.sbuf_tensor(name, shape, dtype))

    BIG = sb("BIG", [P, 32768], BF16)
    QG = sb("QG", [P, 2048], BF16)
    KG = sb("KG", [P, 2048], BF16)
    OSC = sb("OSC", [P, 4096], BF16)
    SX = sb("SX", [P, 1024], F32)
    SBX = sb("SBX", [P, 1024], BF16)
    WS = [sb(f"WS{i}", [P, 4096], BF16) for i in range(4)]
    SCR = [sb(f"SCR{i}", [P, 1024], F32) for i in range(5)]
    BC = [sb(f"BC{i}", [P, 1024], F32) for i in range(3)]
    OS = [sb(f"OS{i}", [P, 1024], BF16) for i in range(6)]
    SBXB = sb("SBXB", [P, 1024], BF16)
    SXB_is_BC1 = True
    TCK = sb("TCK", [P, T], BF16)
    TSK = sb("TSK", [P, T], BF16)
    DMK = sb("DMK", [P, NH * CH], BF16)
    CDT = sb("CDT", [P, NH * CH], BF16)
    KDC = sb("KDC", [P, NH], F32)
    ATW = sb("ATW", [P, ND], F32)
    MLW = sb("MLW", [P, ND], F32)
    GNW = sb("GNW", [P, 4], F32)
    WCB = sb("WCB", [P, 64], F32)
    EYE = sb("EYE", [P, P], BF16)
    ONB = sb("ONB", [P, 2], BF16)
    ONC = sb("ONC", [P, 1], BF16)
    ONR = sb("ONR", [1, P], BF16)
    STC = sb("STC", [P, 2 * NT], F32)
    ROW = sb("ROW", [1, 4096], F32)
    RWB = sb("RWB", [1, 1024], BF16)
    QWB = [sb(f"QWB{i}", [P, 1024], BF16) for i in range(3)]
    QSB = sb("QSB", [P, 1024], BF16)
    QEX = [sb(f"QEX{i}", [P, 1024], BF16) for i in range(2)]
    QRB = sb("QRB", [2, 1024], BF16)
    MGP = sb("MGP", [P, 1], F32)
    MGN = sb("MGN", [P, 1], F32)

    PS = [es.enter_context(nc.psum_tensor(f"PS{i}", [P, 512], F32))
          for i in range(6)]
    PTR = es.enter_context(nc.psum_tensor("PTR", [P, 512], BF16))
    PSR = es.enter_context(nc.psum_tensor("PSR", [P, 512], F32))
    PRA = PSR
    PRB = PS[5]

    sBIG = Slot(); sQG = Slot(); sKG = Slot(); sOSC = Slot()
    sSX = Slot(); sSBX = Slot(); sSXB = Slot(); sSBXB = Slot()
    sW = [Slot() for _ in WS]
    wrr = [0]

    def next_w():
        i = wrr[0] % 4
        wrr[0] += 1
        return i

    prr = [0]

    def next_ps():
        i = prr[0] % 4
        prr[0] += 1
        return i
    sSCR = [Slot() for _ in SCR]
    sBC = [Slot() for _ in BC]
    sOS = [Slot() for _ in OS]
    sPS = [Slot() for _ in PS]
    sPTR = Slot(); sPSR = Slot()
    sPRA = sPSR
    sPRB = sPS[5]
    sROW = Slot(); sRWB = Slot(); sSTC = Slot(); sTAB = Slot()
    sQW = [Slot(), Slot(), Slot()]; sQS = Slot()
    sQE = [Slot(), Slot()]
    sQRB = Slot(); sMG = Slot()
    sRD = [Slot() for _ in range(8)]
    sRB = [Slot() for _ in range(4)]
    sDR = {n: Slot() for n in ("a", "q", "o", "sg", "res2", "u3", "x1",
                               "sloc", "wqgu", "wqp")}

    def dma(qeng, out_ap, in_ap, sem=None, oslot=None, islots=()):
        if oslot is not None:
            sem = oslot.ds
        assert sem is not None
        for s in islots:
            if s.w is not None:
                wt(qeng, *s.w)
        if oslot is not None:
            use_w(qeng, oslot)
        em(qeng, lambda E, o=out_ap, i=in_ap: E.dma_start(out=o, in_=i), inc=sem)
        if oslot is not None:
            mark_w(sem, oslot)
        for s in islots:
            s.r = (sem, cnt[sem])

    def expand_row(src_row_ap, n, dst_ap, dst_slot, rd_i):
        dma("gpsimd", rowdr[rd_i:rd_i + 1, 0:n], src_row_ap, "da",
            oslot=sRD[rd_i], islots=(sROW,))
        src = rowdr[rd_i:rd_i + 1, 0:n].rearrange("r c -> r () c") \
            .broadcast_to([1, P, n])
        dma("gpsimd", dst_ap, src, "da", oslot=dst_slot, islots=(sRD[rd_i],))

    def mm(out_ap, lhsT_ap, rhs_ap, start, stop, inc=None):
        em("tensor",
           lambda E, o=out_ap, l=lhsT_ap, r=rhs_ap, s1=start, s2=stop:
               E.matmul(o, l, r, start=s1, stop=s2), inc=inc)

    def load_panel(wi, dram_ap, r0, r1, c0, c1, qeng="gpsimd"):
        width = c1 - c0
        nk = (r1 - r0) // P
        assert nk * width <= 4096
        src = dram_ap[r0:r1, c0:c1].rearrange("(t p) c -> p t c", p=P)
        dma(qeng, WS[wi][:, 0:nk * width], src, oslot=sW[wi])

    def wtile(wi, k, width, o0, ow_):
        return WS[wi][:, k * width + o0:k * width + o0 + ow_]

    def row_rstd(src_off, dst_off, n=512):
        wt("scalar", "dv")
        em("scalar", lambda E: E.activation(ROW[0:1, src_off:src_off + n],
                                            ROW[0:1, src_off:src_off + n],
                                            AF.Ln))
        em("scalar", lambda E: E.activation(ROW[0:1, dst_off:dst_off + n],
                                            ROW[0:1, src_off:src_off + n],
                                            AF.Exp, scale=-0.5), inc="ac")
        mark_w("ac", sROW)

    for dst, srcp in ((DMK, dmaskT_i), (CDT, cdecT_i), (KDC, kdec_i),
                      (ATW, attw_i), (MLW, mlpw_i), (GNW, gnw_i),
                      (WCB, wcomb_i), (EYE, eye_i), (ONB, onesb_i),
                      (ONC, onesc_i), (ONR, onesr_i)):
        dma("sync", dst[:], srcp, "dtab")
    tab_dw = cnt["dtab"]
    em("vector", lambda E: E.memset(MGP[:], MAGIC))
    em("vector", lambda E: E.memset(MGN[:], -MAGIC), inc="dv")
    mark_w("dv", sMG)

    for t in range(ND):
        sc = sSCR[t % 2]
        dma("sync", SCR[t % 2][:], x0fm[t * P:(t + 1) * P, :], "da", oslot=sc)
        use_r("scalar", sc)
        use_w("scalar", sOS[t % 2], sOS[2 + t % 2])
        em("scalar", lambda E, t=t: E.copy(OS[t % 2][:], SCR[t % 2][:]))
        em("scalar", lambda E, t=t: E.activation(OS[2 + t % 2][:], OS[t % 2][:],
                                                 AF.Square), inc="ac")
        mark_w("ac", sOS[t % 2], sOS[2 + t % 2])
        sc.r = ("ac", cnt["ac"])
        wt("tensor", "dtab", tab_dw)
        use_r("tensor", sOS[t % 2], sOS[2 + t % 2])
        if t == 0:
            use_w("tensor", sPRA, sPRB, sPS[3], sPS[4])
        for sl in range(2):
            pr = (PRA, PRB)[sl]
            mm(pr[0:1, :], ONC[:], OS[t % 2][:, sl * 512:(sl + 1) * 512],
               start=(t == 0), stop=(t == ND - 1))
            pq = (PS[3], PS[4])[sl]
            mm(pq[0:1, :], ONC[:], OS[2 + t % 2][:, sl * 512:(sl + 1) * 512],
               start=(t == 0), stop=(t == ND - 1),
               inc="pe" if sl == 1 else None)
        mark_r("pe", sOS[t % 2], sOS[2 + t % 2])
    mark_w("pe", sPRA, sPRB, sPS[3], sPS[4])

    use_r("vector", sPRA, sPRB, sPS[3], sPS[4])
    use_w("vector", sROW)
    for sl in range(2):
        em("vector", lambda E, sl=sl, pr=(PRA, PRB)[sl]: E.tensor_scalar(
            ROW[0:1, sl * 512:(sl + 1) * 512], pr[0:1, :], 1.0 / D, None,
            ALU.mult))
        em("vector", lambda E, sl=sl, pq=(PS[3], PS[4])[sl]: E.tensor_scalar(
            ROW[0:1, 1024 + sl * 512:1024 + (sl + 1) * 512], pq[0:1, :],
            1.0 / D, None, ALU.mult))
    em("vector", lambda E: E.tensor_tensor(
        ROW[0:1, 3072:4096], ROW[0:1, 0:1024], ROW[0:1, 0:1024], ALU.mult))
    em("vector", lambda E: E.tensor_sub(
        ROW[0:1, 3072:4096], ROW[0:1, 1024:2048], ROW[0:1, 3072:4096]))
    em("vector", lambda E: E.tensor_scalar(
        ROW[0:1, 2048:3072], ROW[0:1, 3072:4096], 1e-5, None, ALU.add), inc="dv")
    mark_w("dv", sROW)
    mark_r("dv", sPRA, sPRB, sPS[3], sPS[4])
    wt("scalar", "dv")
    em("scalar", lambda E: E.activation(ROW[0:1, 2048:3072],
                                        ROW[0:1, 2048:3072], AF.Ln))
    em("scalar", lambda E: E.activation(ROW[0:1, 1024:2048],
                                        ROW[0:1, 2048:3072], AF.Exp,
                                        scale=-0.5))
    em("scalar", lambda E: E.activation(ROW[0:1, 2048:3072],
                                        ROW[0:1, 2048:3072], AF.Exp,
                                        scale=-1.0), inc="ac")
    mark_w("ac", sROW)
    wt("vector", "ac")
    em("vector", lambda E: E.tensor_tensor(
        ROW[0:1, 2048:3072], ROW[0:1, 3072:4096], ROW[0:1, 2048:3072], ALU.mult))
    em("vector", lambda E: E.tensor_scalar(
        ROW[0:1, 2048:3072], ROW[0:1, 2048:3072], 1e-6, None, ALU.add), inc="dv")
    mark_w("dv", sROW)
    row_rstd(2048, 2048, 1024)
    wt("vector", "ac")
    em("vector", lambda E: E.tensor_tensor(
        ROW[0:1, 2048:3072], ROW[0:1, 1024:2048], ROW[0:1, 2048:3072], ALU.mult))
    expand_row(ROW[0:1, 2048:3072], 1024, BC[0][:], sBC[0], 0)

    dma("sync", SCR[2][:], cosraw, "da", oslot=sSCR[2])
    use_r("vector", sSCR[2], sBC[0])
    use_w("vector", sTAB)
    em("vector", lambda E: E.tensor_tensor(TCK[:], SCR[2][:], BC[0][:], ALU.mult),
       inc="dv")
    sSCR[2].r = ("dv", cnt["dv"])
    dma("sync", SCR[2][:], sinraw, "da", oslot=sSCR[2])
    use_r("vector", sSCR[2])
    em("vector", lambda E: E.tensor_tensor(TSK[:], SCR[2][:], BC[0][:], ALU.mult),
       inc="dv")
    mark_w("dv", sTAB)
    sSCR[2].r = ("dv", cnt["dv"])

    use_r("vector", sROW)
    use_w("vector", sRWB)
    em("vector", lambda E: E.tensor_copy(RWB[0:1, :], ROW[0:1, 2048:3072]),
       inc="dv")
    mark_w("dv", sRWB)
    use_r("tensor", sRWB)
    use_w("tensor", sPTR)
    for tt in range(NT):
        em("tensor", lambda E, tt=tt: E.transpose(
            PTR[:, 2 * tt:2 * tt + 1], RWB[0:1, tt * P:(tt + 1) * P],
            EYE[0:1, 0:1]), inc="pe" if tt == NT - 1 else None)
    mark_w("pe", sPTR)
    mark_r("pe", sRWB)
    use_r("vector", sPTR)
    use_w("vector", sSTC)
    em("vector", lambda E: E.tensor_copy(STC[:], PTR[:, 0:2 * NT]), inc="dv")
    mark_w("dv", sSTC)
    sPTR.r = ("dv", cnt["dv"])

    expand_row(ROW[0:1, 0:1024], 1024, BC[1][:], sBC[1], 2)
    expand_row(ROW[0:1, 1024:2048], 1024, BC[2][:], sBC[2], 3)

    for t in range(ND):
        sc = sSCR[t % 2]
        dma("sync", SCR[t % 2][:], x0fm[t * P:(t + 1) * P, :], "da", oslot=sc)
        use_r("vector", sc, sBC[1], sBC[2])
        use_w("vector", sSCR[2 + t % 2], sOS[t % 2], sBIG)
        em("vector", lambda E, t=t: E.tensor_sub(
            SCR[2 + t % 2][:], SCR[t % 2][:], BC[1][:]))
        em("vector", lambda E, t=t: E.tensor_tensor(
            OS[t % 2][:], SCR[2 + t % 2][:], BC[2][:], ALU.mult))
        em("vector", lambda E, t=t: E.tensor_scalar(
            BIG[:, t * 1024:(t + 1) * 1024], SCR[2 + t % 2][:],
            ATW[:, t:t + 1], None, ALU.mult), inc="dv")
        mark_w("dv", sOS[t % 2], sBIG, sSCR[2 + t % 2])
        sc.r = ("dv", cnt["dv"])
        dma("scalar", a_scr[t * P:(t + 1) * P, :], OS[t % 2][:], "da",
            oslot=sDR["a"], islots=(sOS[t % 2],))

    barrier()

    def quant_segA_load(src_ap, wb):
        dma("gpsimd", QWB[wb][:], src_ap, oslot=sQW[wb])

    def quant_segA(wb, eb, wsl):
        use_r("scalar", sQW[wb])
        use_w("scalar", sQE[eb])
        em("scalar", lambda E, wb=wb, eb=eb: E.activation(
            QEX[eb][:], QWB[wb][:], AF.Abs), inc="ac")
        mark_w("ac", sQE[eb])
        use_r("tensor", sQE[eb])
        nsl = 1024 // wsl
        for sl in range(nsl):
            use_w("tensor", sPS[4])
            mm(PS[4][0:2, 0:wsl], ONB[:], QEX[eb][:, sl * wsl:(sl + 1) * wsl],
               start=True, stop=True, inc="pe")
            mark_w("pe", sPS[4])
            use_r("vector", sPS[4])
            use_w("vector", sQRB)
            em("vector", lambda E, sl=sl, w=wsl: E.tensor_scalar(
                QRB[0:2, sl * w:(sl + 1) * w], PS[4][0:2, 0:w], 1.0 / 64.0,
                1e-8, ALU.mult, ALU.max), inc="dv")
            mark_w("dv", sQRB)
            mark_r("dv", sPS[4])
        mark_r("pe", sQE[eb])
        dma("scalar", rowbf[0:2, 0:1024], QRB[0:2, 0:1024], oslot=sRB[0],
            islots=(sQRB,))
        brd = rowbf[0:2, 0:1024].rearrange("g c -> g () c") \
            .broadcast_to([2, 64, 1024])
        dma("scalar", QEX[eb][:], brd, oslot=sQE[eb], islots=(sRB[0],))

    def quant_segB1(eb):
        use_r("vector", sQE[eb], sMG)
        use_w("vector", sBC[1])
        em("vector", lambda E, eb=eb: E.reciprocal(BC[1][:], QEX[eb][:]),
           inc="dv")
        mark_w("dv", sBC[1])

    def quant_segB2(stage_ap, wb, eb, wname):
        b = wb
        use_r("gpsimd", sQW[b], sBC[1])
        use_w("gpsimd", sQS)
        em("gpsimd", lambda E, b=b: E.tensor_tensor(
            QSB[:], QWB[b][:], BC[1][:], ALU.mult), inc="gp")
        mark_w("gp", sQS)
        sBC[1].r = ("gp", cnt["gp"])
        use_r("scalar", sQS, sMG)
        use_w("scalar", sBC[2])
        em("scalar", lambda E: E.activation(BC[2][:], QSB[:], AF.Relu,
                                            bias=MGP[:]), inc="ac")
        mark_w("ac", sBC[2])
        use_r("vector", sBC[2])
        use_w("vector", sQS)
        em("vector", lambda E: E.tensor_scalar(
            QSB[:], BC[2][:], -MAGIC, 1.0, ALU.add, ALU.min))
        em("vector", lambda E: E.tensor_scalar(
            QSB[:], QSB[:], -1.0, None, ALU.max), inc="dv")
        mark_w("dv", sQS)
        sBC[2].r = ("dv", cnt["dv"])
        use_r("gpsimd", sQS, sQE[eb])
        use_w("gpsimd", sQW[b])
        em("gpsimd", lambda E, b=b, eb=eb: E.tensor_tensor(
            QWB[b][:], QSB[:], QEX[eb][:], ALU.mult), inc="gp")
        mark_w("gp", sQW[b])
        sQS.r = ("gp", cnt["gp"])
        sQE[eb].r = ("gp", cnt["gp"])
        dma("scalar", stage_ap, QWB[b][:], oslot=sDR[wname], islots=(sQW[b],))

    qspecs = []
    for rt in range(ND):
        for cc in range(2):
            qspecs.append((wgu_sh[rt * P:(rt + 1) * P,
                                  cc * 1024:(cc + 1) * 1024],
                           wqgu_st[rt * P:(rt + 1) * P,
                                   cc * 1024:(cc + 1) * 1024], 512, "wqgu"))
    for bt in range(ND):
        qspecs.append((wp_sh[bt * 512:(bt + 1) * 512, 0:256]
                       .rearrange("(t p) c -> p t c", p=P),
                       wqp_st[bt * 512:(bt + 1) * 512, 0:256]
                       .rearrange("(t p) c -> p t c", p=P), 256, "wqp"))
    qstate = {"a": 0, "b1": 0, "b2": 0}

    def pop_quant(n):
        for _ in range(n):
            if qstate["a"] < len(qspecs):
                quant_segA_load(qspecs[qstate["a"]][0], qstate["a"] % 3)
            if qstate["b2"] < qstate["b1"]:
                i = qstate["b2"]
                quant_segB2(qspecs[i][1], i % 3, i % 2, qspecs[i][3])
                qstate["b2"] += 1
            if qstate["b1"] < qstate["a"]:
                i = qstate["b1"]
                quant_segB1(i % 2)
                qstate["b1"] += 1
            if qstate["a"] < len(qspecs):
                i = qstate["a"]
                quant_segA(i % 3, i % 2, qspecs[i][2])
                qstate["a"] += 1

    def flush_quant():
        while (qstate["b2"] < qstate["b1"] or qstate["b1"] < qstate["a"]
               or qstate["a"] < len(qspecs)):
            pop_quant(1)

    def gproj_panel(pan):
        wg = next_w()
        load_panel(wg, gw, 0, D, pan * 256, (pan + 1) * 256)
        use_r("tensor", sW[wg], sBIG)
        for sub in range(2):
            gt = pan * 2 + sub
            for ts in range(2):
                pp_ = next_ps()
                use_w("tensor", sPS[pp_])
                for k in range(ND):
                    mm(PS[pp_][:], wtile(wg, k, 256, sub * 128, P),
                       BIG[:, k * 1024 + ts * 512:k * 1024 + (ts + 1) * 512],
                       start=(k == 0), stop=(k == ND - 1),
                       inc="pe" if k == ND - 1 else None)
                mark_w("pe", sPS[pp_])
                use_r("vector", sPS[pp_], sBC[0])
                use_w("vector", sSCR[ts])
                em("vector", lambda E, ts=ts, pp_=pp_: E.tensor_tensor(
                    SCR[ts][:, 0:512], PS[pp_][:],
                    BC[0][:, ts * 512:(ts + 1) * 512], ALU.mult), inc="dv")
                mark_w("dv", sSCR[ts])
                mark_r("dv", sPS[pp_])
                use_r("scalar", sSCR[ts])
                use_w("scalar", sOS[ts])
                em("scalar", lambda E, ts=ts: E.activation(
                    OS[ts][:, 0:512], SCR[ts][:, 0:512], AF.Silu), inc="ac")
                mark_w("ac", sOS[ts])
                sSCR[ts].r = ("ac", cnt["ac"])
                dma("scalar", sg_scr[gt * P:(gt + 1) * P, ts * 512:(ts + 1) * 512],
                    OS[ts][:, 0:512], "da", oslot=sDR["sg"], islots=(sOS[ts],))
            if sub == 0 and pan % 2 == 0:
                pop_quant(1)
        sW[wg].r = ("pe", cnt["pe"])


    for gp2 in range(NH // 2):
        for lane in range(2):
            h = 2 * gp2 + lane
            qdst = QG if lane == 0 else None
            for which, wdram in ((0, qw), (1, kw)):
                wsl_ = next_w()
                load_panel(wsl_, wdram, 0, D, 256 * h, 256 * (h + 1))
                use_r("tensor", sW[wsl_], sBIG)
                if lane == 0:
                    dst = (QG, KG)[which]
                    dsts = (sQG, sKG)[which]
                    dof = 0
                else:
                    dst = OSC
                    dsts = sOSC
                    dof = which * 2048
                scl = (1.0 / 16.0) if which == 0 else 1.0
                for ts in range(2):
                    pa, pb = next_ps(), next_ps()
                    use_w("tensor", sPS[pa], sPS[pb])
                    for half in range(2):
                        pt = (pa, pb)[half]
                        for k in range(ND):
                            mm(PS[pt][:], wtile(wsl_, k, 256, half * 128, P),
                               BIG[:, k * 1024 + ts * 512:k * 1024 + (ts + 1) * 512],
                               start=(k == 0), stop=(k == ND - 1),
                               inc="pe" if (k == ND - 1 and half == 1) else None)
                    mark_w("pe", sPS[pa], sPS[pb])
                    use_r("vector", sPS[pa], sPS[pb], sTAB)
                    sc0, sc1 = sSCR[ts % 2], sSCR[2 + ts % 2]
                    c0_ = SCR[ts % 2][:, 0:512]
                    c1_ = SCR[2 + ts % 2][:, 0:512]
                    use_w("vector", sc0, sc1, dsts)
                    cs = TCK[:, ts * 512:(ts + 1) * 512]
                    sn = TSK[:, ts * 512:(ts + 1) * 512]
                    d0 = dst[:, dof + ts * 512:dof + (ts + 1) * 512]
                    d1 = dst[:, dof + 1024 + ts * 512:dof + 1024 + (ts + 1) * 512]
                    em("vector", lambda E, cs=cs, a=c0_, pa=pa, s=scl:
                       E.scalar_tensor_tensor(a, PS[pa][:], s, cs, ALU.mult,
                                              ALU.mult))
                    em("vector", lambda E, sn=sn, b=c1_, pb=pb, s=scl:
                       E.scalar_tensor_tensor(b, PS[pb][:], s, sn, ALU.mult,
                                              ALU.mult))
                    em("vector", lambda E, d0=d0, a=c0_, b=c1_: E.tensor_sub(
                        d0, a, b))
                    em("vector", lambda E, cs=cs, a=c0_, pb=pb, s=scl:
                       E.scalar_tensor_tensor(a, PS[pb][:], s, cs, ALU.mult,
                                              ALU.mult))
                    em("vector", lambda E, sn=sn, b=c1_, pa=pa, s=scl:
                       E.scalar_tensor_tensor(b, PS[pa][:], s, sn, ALU.mult,
                                              ALU.mult))
                    em("vector", lambda E, d1=d1, a=c0_, b=c1_: E.tensor_add(
                        d1, a, b), inc="dv")
                    mark_w("dv", dsts, sc0, sc1)
                    mark_r("dv", sPS[pa], sPS[pb])
                sW[wsl_].r = ("pe", cnt["pe"])
            qsrc = QG if lane == 0 else OSC
            for i in range(2):
                dma("scalar", q_scr[(h * 2 + i) * P:(h * 2 + i + 1) * P, :],
                    qsrc[:, i * 1024:(i + 1) * 1024], "da", oslot=sDR["q"],
                    islots=(sQG if lane == 0 else sOSC,))
            va, vb = next_w(), next_w()
            load_panel(va, vw, 0, D, 512 * h, 512 * h + 256)
            load_panel(vb, vw, 0, D, 512 * h + 256, 512 * (h + 1))
            use_r("tensor", sW[va], sW[vb], sBIG, sSTC)
            for tt in range(NT):
                pv = next_ps()
                use_w("tensor", sPS[pv])
                for k in range(ND):
                    lt = BIG[:, k * 1024 + tt * 128:k * 1024 + (tt + 1) * 128]
                    mm(PS[pv][:, 0:256], lt, wtile(va, k, 256, 0, 256),
                       start=(k == 0), stop=False)
                    mm(PS[pv][:, 256:512], lt, wtile(vb, k, 256, 0, 256),
                       start=False, stop=(k == ND - 1),
                       inc="pe" if k == ND - 1 else None)
                mark_w("pe", sPS[pv])
                use_r("vector", sPS[pv], sSTC)
                use_w("vector", sBIG)
                dv_ = BIG[:, 16384 + lane * 4096 + tt * 512:
                          16384 + lane * 4096 + (tt + 1) * 512]
                em("vector", lambda E, d=dv_, tt=tt, pv=pv: E.tensor_scalar(
                    d, PS[pv][:], STC[:, 2 * tt:2 * tt + 1], None, ALU.mult),
                    inc="dv")
                mark_w("dv", sBIG)
                mark_r("dv", sPS[pv])
            sW[va].r = ("pe", cnt["pe"])
            sW[vb].r = ("pe", cnt["pe"])

        LA = dict(h=2 * gp2, qb=QG, qof=0, kb=KG, kof=0, sq=sQG, sk=sKG,
                  pa=0, po=2, vof=0, obs=(0, 1), ptof=0, ost=4,
                  SXt=SX, sSXt=sSX, SBt=SBX, sSBt=sSBX)
        LB = dict(h=2 * gp2 + 1, qb=OSC, qof=0, kb=OSC, kof=2048, sq=sOSC,
                  sk=sOSC, pa=1, po=3, vof=4096, obs=(2, 3), ptof=256, ost=5,
                  SXt=BC[1], sSXt=sSXB, SBt=SBXB, sSBt=sSBXB)

        def stage_front(L, n):
            h = L["h"]
            ob = OS[L["obs"][n % 2]]
            sob = sOS[L["obs"][n % 2]]
            pa = PS[L["pa"]]
            spa = sPS[L["pa"]]
            qcs = [L["qb"][:, L["qof"] + i * 1024 + n * 128:
                           L["qof"] + i * 1024 + (n + 1) * 128] for i in range(2)]
            kcs = [L["kb"][:, L["kof"] + i * 1024 + n * 128:
                           L["kof"] + i * 1024 + (n + 1) * 128] for i in range(2)]
            use_r("tensor", L["sq"], L["sk"])
            use_w("tensor", spa)
            for i in range(2):
                mm(pa[:, 0:128], kcs[i], qcs[i], start=(i == 0),
                   stop=(i == 1), inc="pe" if i == 1 else None)
            mark_w("pe", spa)
            use_w("tensor", sPTR)
            for i in range(2):
                em("tensor", lambda E, i=i, kk=kcs[i], o=L["ptof"]: E.transpose(
                    PTR[:, o + i * 128:o + (i + 1) * 128], kk, EYE[:]),
                    inc="pe" if i == 1 else None)
            mark_w("pe", sPTR)
            use_r("vector", spa, sPTR)
            use_w("vector", sob)
            em("vector", lambda E, ob=ob, pa=pa, h=h: E.tensor_tensor(
                ob[:, 0:128], pa[:, 0:128],
                DMK[:, h * 128:(h + 1) * 128], ALU.mult), inc="dv")
            mark_r("dv", spa)
            for i in range(2):
                em("vector", lambda E, ob=ob, i=i, qq=qcs[i], h=h:
                   E.tensor_tensor(
                       ob[:, 128 + i * 128:128 + (i + 1) * 128], qq,
                       CDT[:, h * 128:(h + 1) * 128], ALU.mult))
            em("vector", lambda E, ob=ob, h=h, o=L["ptof"]: E.tensor_scalar(
                ob[:, 384:640], PTR[:, o:o + 256], KDC[:, h:h + 1], None,
                ALU.mult), inc="dv")
            mark_w("dv", sob)
            sPTR.r = ("dv", cnt["dv"])

        def stage_back(L, n):
            h = L["h"]
            ob = OS[L["obs"][n % 2]]
            sob = sOS[L["obs"][n % 2]]
            po = PS[L["po"]]
            spo = sPS[L["po"]]
            SXt, sSXt, SBt, sSBt = L["SXt"], L["sSXt"], L["SBt"], L["sSBt"]
            vch = BIG[:, 16384 + L["vof"] + n * 512:
                      16384 + L["vof"] + (n + 1) * 512]
            use_r("tensor", sob, sBIG, sSBt)
            use_w("tensor", spo)
            for e in range(4):
                last = (n == 0 and e == 3)
                mm(po[:, e * 128:(e + 1) * 128],
                   vch[:, e * 128:(e + 1) * 128], ob[:, 0:128],
                   start=(e == 0), stop=last, inc="pe" if last else None)
            if n > 0:
                for e in range(4):
                    for i in range(2):
                        last = (e == 3 and i == 1)
                        mm(po[:, e * 128:(e + 1) * 128],
                           SBt[:, i * 512 + e * 128:i * 512 + (e + 1) * 128],
                           ob[:, 128 + i * 128:128 + (i + 1) * 128],
                           start=False, stop=last,
                           inc="pe" if last else None)
            mark_w("pe", spo)
            mark_r("pe", sSBt)
            use_w("tensor", sPS[4], sPS[5])
            for kt in range(2):
                mm(PS[4 + kt][:], ob[:, 384 + kt * 128:384 + (kt + 1) * 128],
                   vch, start=True, stop=True,
                   inc="pe" if kt == 1 else None)
            mark_w("pe", sPS[4], sPS[5])
            mark_r("pe", sob)
            ost, sost = OS[L["ost"]], sOS[L["ost"]]
            use_r("vector", spo)
            use_w("vector", sost)
            em("vector", lambda E, po=po, o=ost: E.tensor_copy(
                o[:, 0:512], po[:, 0:512]), inc="dv")
            mark_w("dv", sost)
            mark_r("dv", spo)
            oout = o_scr[(h * 4) * P:(h * 4 + 4) * P, n * 128:(n + 1) * 128] \
                .rearrange("(e p) c -> p e c", p=P)
            dma("scalar", oout, ost[:, 0:512], "da", oslot=sDR["o"],
                islots=(sost,))
            use_r("vector", sPS[4], sPS[5])
            use_w("vector", sSXt, sSBt)
            gch = float(GAMMA[h]) ** CH
            for kt in range(2):
                sx_sl = SXt[:, kt * 512:(kt + 1) * 512]
                sb_sl = SBt[:, kt * 512:(kt + 1) * 512]
                if n == 0:
                    em("vector", lambda E, s=sx_sl, kt=kt: E.tensor_copy(
                        s, PS[4 + kt][:]))
                else:
                    em("vector", lambda E, s=sx_sl, kt=kt, gch=gch:
                       E.scalar_tensor_tensor(s, s, gch, PS[4 + kt][:],
                                              ALU.mult, ALU.add))
                em("vector", lambda E, s=sx_sl, b=sb_sl: E.tensor_copy(b, s),
                   inc="dv" if kt == 1 else None)
            mark_w("dv", sSXt, sSBt)
            mark_r("dv", sPS[4], sPS[5])

        stage_front(LA, 0)
        stage_front(LB, 0)
        for n in range(1, NCH):
            stage_front(LA, n)
            stage_back(LA, n - 1)
            stage_front(LB, n)
            stage_back(LB, n - 1)
        stage_back(LA, NCH - 1)
        stage_back(LB, NCH - 1)
        for L in (LA, LB):
            for kt in range(2):
                dma("scalar",
                    sloc_scr[(L["h"] * 2 + kt) * P:(L["h"] * 2 + kt + 1) * P, :],
                    L["SBt"][:, kt * 512:(kt + 1) * 512], "da",
                    oslot=sDR["sloc"], islots=(L["sSBt"],))
        mark_r("pe", sQG, sKG, sBIG, sOSC)
    barrier()

    if sDR["sloc"].w is not None:
        wt("gpsimd", *sDR["sloc"].w)
    em("gpsimd", lambda E: E.collective_compute(
        "AllGather", ALU.bypass, replica_groups=[[0, 1, 2, 3], [4, 5, 6, 7]],
        ins=[sloc_scr.opt()], outs=[g1out.opt()]), inc="gp")
    cc1 = cnt["gp"]

    for pan in range(16):
        gproj_panel(pan)
    barrier()




    wt("sync", "gp", cc1)
    for h in range(NH):
        for kt in range(2):
            for j in range(3):
                dma("sync", OS[kt * 3 + j][:, 0:512],
                    g1out[j * 2048 + h * 256 + kt * 128:
                          j * 2048 + h * 256 + (kt + 1) * 128, :],
                    "da", oslot=sOS[kt * 3 + j])
        for kt in range(2):
            acc = SCR[kt][:, 0:512]
            sacc = sSCR[kt]
            for j in range(3):
                o_ = OS[kt * 3 + j]
                use_r("vector", sOS[kt * 3 + j])
                use_w("vector", sacc)
                if j == 0:
                    em("vector", lambda E, a=acc, j=j, h=h, o_=o_:
                       E.tensor_scalar(a, o_[:, 0:512],
                                       WCB[:, j * 8 + h:j * 8 + h + 1],
                                       None, ALU.mult), inc="dv")
                else:
                    em("vector", lambda E, a=acc, j=j, h=h, o_=o_:
                       E.scalar_tensor_tensor(
                           a, o_[:, 0:512], WCB[:, j * 8 + h:j * 8 + h + 1],
                           a, ALU.mult, ALU.add), inc="dv")
                mark_w("dv", sacc)
                sOS[kt * 3 + j].r = ("dv", cnt["dv"])
            use_w("vector", sSBX)
            em("vector", lambda E, kt=kt, a=acc: E.tensor_copy(
                SBX[:, kt * 512:(kt + 1) * 512], a), inc="dv")
            mark_w("dv", sSBX)
            sacc.r = ("dv", cnt["dv"])
        for i in range(2):
            dma("sync", QG[:, i * 1024:(i + 1) * 1024],
                q_scr[(h * 2 + i) * P:(h * 2 + i + 1) * P, :], "da",
                oslot=sQG, islots=(sDR["q"],))
        dma("sync", OS[0][:, :], gamtab_i[h * P:(h + 1) * P, :], "da",
            oslot=sOS[0])
        use_r("vector", sQG, sOS[0])
        use_w("vector", sQG)
        for i in range(2):
            em("vector", lambda E, i=i: E.tensor_tensor(
                QG[:, i * 1024:(i + 1) * 1024], QG[:, i * 1024:(i + 1) * 1024],
                OS[0][:], ALU.mult), inc="dv" if i == 1 else None)
        mark_w("dv", sQG)
        sOS[0].r = ("dv", cnt["dv"])
        for ts in range(2):
            cb = KG if ts == 0 else QG
            scb = sKG if ts == 0 else sQG
            sgslots = [(OSC, sOSC, e * 512) if ts == 0 else
                       (OS[e], sOS[e], 0) for e in range(4)]
            for e in range(4):
                dma("sync", OSC[:, (4 + e) * 512:(5 + e) * 512],
                    o_scr[(h * 4 + e) * P:(h * 4 + e + 1) * P,
                          ts * 512:(ts + 1) * 512], "da", oslot=sOSC,
                    islots=(sDR["o"],))
                sgb, ssgb, sgo = sgslots[e]
                dma("sync", sgb[:, sgo:sgo + 512],
                    sg_scr[(h * 4 + e) * P:(h * 4 + e + 1) * P,
                           ts * 512:(ts + 1) * 512], "da", oslot=ssgb,
                    islots=(sDR["sg"],))
            use_r("tensor", sSBX, sQG)
            use_w("tensor", sPS[0], sPS[1], sPS[2], sPS[3])
            for e in range(4):
                for i in range(2):
                    mm(PS[e][:],
                       SBX[:, i * 512 + e * 128:i * 512 + (e + 1) * 128],
                       QG[:, i * 1024 + ts * 512:i * 1024 + (ts + 1) * 512],
                       start=(i == 0), stop=(i == 1),
                       inc="pe" if (e == 3 and i == 1) else None)
            mark_w("pe", sPS[0], sPS[1], sPS[2], sPS[3])
            if ts == 1:
                mark_r("pe", sSBX, sQG)
            for e in range(4):
                use_r("vector", sOSC, sPS[e])
                use_w("vector", scb)
                em("vector", lambda E, e=e, cb=cb: E.tensor_add(
                    cb[:, e * 512:(e + 1) * 512],
                    OSC[:, (4 + e) * 512:(5 + e) * 512], PS[e][:]),
                    inc="dv")
                mark_w("dv", scb)
                mark_r("dv", sPS[e])
                use_r("scalar", scb)
                sqb = OS[4 + e % 2]
                use_w("scalar", sOS[4 + e % 2])
                em("scalar", lambda E, e=e, cb=cb, q=sqb: E.activation(
                    q[:, 0:512], cb[:, e * 512:(e + 1) * 512], AF.Square),
                    inc="ac")
                mark_w("ac", sOS[4 + e % 2])
                use_r("tensor", sOS[4 + e % 2])
                if e == 0:
                    use_w("tensor", sPRA if ts == 0 else sPRB)
                pr = PRA if ts == 0 else PRB
                mm(pr[0:1, :], ONC[:], sqb[:, 0:512], start=(e == 0),
                   stop=(e == 3), inc="pe")
                mark_r("pe", sOS[4 + e % 2])
            sOSC.r = ("dv", cnt["dv"])
            if ts == 0:
                mark_w("pe", sPRA)
            else:
                mark_w("pe", sPRB)
        use_r("vector", sPRA, sPRB)
        use_w("vector", sROW)
        em("vector", lambda E: E.tensor_scalar(
            ROW[0:1, 0:512], PRA[0:1, :], 1.0 / DV, 1e-6, ALU.mult, ALU.add))
        em("vector", lambda E: E.tensor_scalar(
            ROW[0:1, 512:1024], PRB[0:1, :], 1.0 / DV, 1e-6, ALU.mult,
            ALU.add), inc="dv")
        mark_w("dv", sROW)
        mark_r("dv", sPRA, sPRB)
        row_rstd(0, 1024, 1024)
        wt("vector", "ac")
        use_w("vector", sRWB)
        em("vector", lambda E: E.tensor_copy(RWB[0:1, 0:1024],
                                             ROW[0:1, 1024:2048]), inc="dv")
        mark_w("dv", sRWB)
        use_r("tensor", sRWB)
        use_w("tensor", sPS[4], sPS[5])
        mm(PS[4][:], ONR[:], RWB[0:1, 0:512], start=True, stop=True)
        mm(PS[5][:], ONR[:], RWB[0:1, 512:1024], start=True, stop=True,
           inc="pe")
        mark_w("pe", sPS[4], sPS[5])
        mark_r("pe", sRWB)
        for ts in range(2):
            cb = KG if ts == 0 else QG
            scb = sKG if ts == 0 else sQG
            pb = PS[4] if ts == 0 else PS[5]
            spb = sPS[4] if ts == 0 else sPS[5]
            for e in range(4):
                tb = OS[4 + e % 2]
                stb = sOS[4 + e % 2]
                use_r("vector", scb, spb)
                use_w("vector", stb)
                em("vector", lambda E, e=e, cb=cb, pb=pb, t=tb:
                   E.scalar_tensor_tensor(
                       t[:, 0:512], cb[:, e * 512:(e + 1) * 512],
                       GNW[:, e:e + 1], pb[:], ALU.mult, ALU.mult), inc="dv")
                mark_w("dv", stb)
                if e == 3:
                    mark_r("dv", spb, scb)
                sgb = OSC[:, e * 512:(e + 1) * 512] if ts == 0 \
                    else OS[e][:, 0:512]
                ssgb = sOSC if ts == 0 else sOS[e]
                use_r("vector", stb, ssgb)
                use_w("vector", sBIG)
                em("vector", lambda E, e=e, h=h, ts=ts, t=tb, s=sgb:
                   E.tensor_tensor(
                       BIG[:, (h * 4 + e) * 1024 + ts * 512:
                           (h * 4 + e) * 1024 + (ts + 1) * 512],
                       t[:, 0:512], s, ALU.mult), inc="dv")
                mark_w("dv", sBIG)
                stb.r = ("dv", cnt["dv"])
                ssgb.r = ("dv", cnt["dv"])
        pop_quant(1)


    pend = []
    for pan in range(8):
        ha, hb = next_w(), next_w()
        load_panel(ha, ow, 0, D, pan * 256, (pan + 1) * 256)
        load_panel(hb, ow, D, 2 * D, pan * 256, (pan + 1) * 256)
        use_r("tensor", sW[ha], sW[hb], sBIG)
        for sub in range(2):
            dt = pan * 2 + sub
            for ts in range(2):
                dma("sync", OS[ts][:, 0:512],
                    a_scr[dt * P:(dt + 1) * P, ts * 512:(ts + 1) * 512], "da",
                    oslot=sOS[ts], islots=(sDR["a"],))
                ph = next_ps()
                use_w("tensor", sPS[ph])
                for k in range(ND):
                    mm(PS[ph][:], wtile(ha, k, 256, sub * 128, P),
                       BIG[:, k * 1024 + ts * 512:k * 1024 + (ts + 1) * 512],
                       start=(k == 0), stop=False)
                for k in range(ND):
                    mm(PS[ph][:], wtile(hb, k, 256, sub * 128, P),
                       BIG[:, (16 + k) * 1024 + ts * 512:
                           (16 + k) * 1024 + (ts + 1) * 512],
                       start=False, stop=(k == ND - 1),
                       inc="pe" if k == ND - 1 else None)
                mark_w("pe", sPS[ph])
                use_r("vector", sPS[ph], sOS[ts])
                use_w("vector", sSCR[ts], sOS[2 + ts])
                em("vector", lambda E, ts=ts, ph=ph: E.tensor_add(
                    SCR[ts][:, 0:512], OS[ts][:, 0:512], PS[ph][:]))
                em("vector", lambda E, ts=ts, dt=dt: E.tensor_scalar(
                    OS[2 + ts][:, 0:512], SCR[ts][:, 0:512], MLW[:, dt:dt + 1],
                    None, ALU.mult), inc="dv")
                mark_w("dv", sSCR[ts], sOS[2 + ts])
                mark_r("dv", sPS[ph])
                sOS[ts].r = ("dv", cnt["dv"])
                dma("scalar", res2_scr[dt * P:(dt + 1) * P, ts * 512:(ts + 1) * 512],
                    SCR[ts][:, 0:512], "da", oslot=sDR["res2"],
                    islots=(sSCR[ts],))
                dma("scalar", u3_scr[dt * P:(dt + 1) * P, ts * 512:(ts + 1) * 512],
                    OS[2 + ts][:, 0:512], "da", oslot=sDR["u3"],
                    islots=(sOS[2 + ts],))
                kq = (2 * (dt % 2) + ts)
                use_r("scalar", sSCR[ts])
                use_w("scalar", sKG)
                em("scalar", lambda E, ts=ts, kq=kq: E.activation(
                    KG[:, kq * 512:(kq + 1) * 512], SCR[ts][:, 0:512],
                    AF.Square), inc="ac")
                mark_w("ac", sKG)
                sSCR[ts].r = ("ac", cnt["ac"])

                def _stats3(dt=dt, ts=ts, kq=kq):
                    use_r("tensor", sKG)
                    if dt == 0 and ts == 0:
                        use_w("tensor", sPRA, sPRB)
                    pr = PRA if ts == 0 else PRB
                    mm(pr[0:1, :], ONC[:], KG[:, kq * 512:(kq + 1) * 512],
                       start=(dt == 0), stop=(dt == ND - 1), inc="pe")
                    mark_r("pe", sKG)
                pend.append(_stats3)
                if len(pend) > 2:
                    pend.pop(0)()
                pop_quant(1)
        sW[ha].r = ("pe", cnt["pe"])
        sW[hb].r = ("pe", cnt["pe"])
    while pend:
        pend.pop(0)()
    flush_quant()
    if sDR["wqgu"].w is not None:
        wt("gpsimd", *sDR["wqgu"].w)
    em("gpsimd", lambda E: E.collective_compute(
        "AllGather", ALU.bypass, replica_groups=[list(range(NCORE))],
        ins=[wqgu_st.opt()], outs=[g2gu.opt()]), inc="gp")
    mark_w("pe", sPRA, sPRB)
    use_r("vector", sPRA, sPRB)
    use_w("vector", sROW)
    for ts in range(2):
        em("vector", lambda E, ts=ts, pr=(PRA, PRB)[ts]: E.tensor_scalar(
            ROW[0:1, ts * 512:(ts + 1) * 512], pr[0:1, :], 1.0 / D, 1e-6,
            ALU.mult, ALU.add), inc="dv" if ts == 1 else None)
    mark_w("dv", sROW)
    mark_r("dv", sPRA, sPRB)
    row_rstd(0, 1024, 1024)




    def mlp_half(H):
        pend2 = []
        for t in range(ND):
            dma("sync", BIG[:, 22528 + t * 512:22528 + (t + 1) * 512],
                u3_scr[t * P:(t + 1) * P, H * 512:(H + 1) * 512], "da",
                oslot=sBIG, islots=(sDR["u3"],))
        expand_row(ROW[0:1, 1024 + H * 512:1024 + (H + 1) * 512], 512,
                   BC[1][:, 0:512], sBC[1], 4)
        for pp in range(22):
            wg_, wy_ = next_w(), next_w()
            load_panel(wg_, gatew, 0, D, pp * 256, (pp + 1) * 256)
            load_panel(wy_, gatew, 0, D, INTER + pp * 256, INTER + (pp + 1) * 256)
            use_r("tensor", sW[wg_], sW[wy_], sBIG)
            for sub in range(2):
                st = pp * 2 + sub
                pg_, py_ = next_ps(), next_ps()
                for which, ww, pb in ((0, wg_, pg_), (1, wy_, py_)):
                    use_w("tensor", sPS[pb])
                    for k in range(ND):
                        mm(PS[pb][:], wtile(ww, k, 256, sub * 128, P),
                           BIG[:, 22528 + k * 512:22528 + (k + 1) * 512],
                           start=(k == 0), stop=(k == ND - 1),
                           inc="pe" if k == ND - 1 else None)
                    mark_w("pe", sPS[pb])
                use_r("vector", sPS[pg_], sPS[py_], sBC[1])
                use_w("vector", sSCR[0], sSCR[1])
                em("vector", lambda E, pg_=pg_: E.tensor_tensor(
                    SCR[0][:, 0:512], PS[pg_][:], BC[1][:, 0:512], ALU.mult))
                em("vector", lambda E, py_=py_: E.tensor_tensor(
                    SCR[1][:, 0:512], PS[py_][:], BC[1][:, 0:512], ALU.mult),
                    inc="dv")
                mark_w("dv", sSCR[0], sSCR[1])
                mark_r("dv", sPS[pg_], sPS[py_])
                use_r("scalar", sSCR[0])
                use_w("scalar", sOS[0])
                em("scalar", lambda E: E.activation(OS[0][:, 0:512],
                                                    SCR[0][:, 0:512], AF.Silu),
                   inc="ac")
                mark_w("ac", sOS[0])
                sSCR[0].r = ("ac", cnt["ac"])
                use_r("vector", sOS[0], sSCR[1])
                use_w("vector", sBIG)
                em("vector", lambda E, st=st: E.tensor_tensor(
                    BIG[:, st * 512:(st + 1) * 512], OS[0][:, 0:512],
                    SCR[1][:, 0:512], ALU.mult), inc="dv")
                mark_w("dv", sBIG)
                sOS[0].r = ("dv", cnt["dv"])
                sSCR[1].r = ("dv", cnt["dv"])
            sW[wg_].r = ("pe", cnt["pe"])
            sW[wy_].r = ("pe", cnt["pe"])
        for dpan in range(8):
            wa_, wb_, wc_ = next_w(), next_w(), next_w()
            load_panel(wa_, downw, 0, D, dpan * 256, (dpan + 1) * 256)
            load_panel(wb_, downw, D, 2 * D, dpan * 256, (dpan + 1) * 256)
            load_panel(wc_, downw, 2 * D, INTER, dpan * 256, (dpan + 1) * 256)
            use_r("tensor", sW[wa_], sW[wb_], sW[wc_], sBIG)
            for sub in range(2):
                dt = dpan * 2 + sub
                dma("sync", SCR[2][:, 0:512],
                    res2_scr[dt * P:(dt + 1) * P, H * 512:(H + 1) * 512], "da",
                    oslot=sSCR[2], islots=(sDR["res2"],))
                dma("sync", SCR[3][:, 0:512],
                    x0fm[dt * P:(dt + 1) * P, H * 512:(H + 1) * 512], "da",
                    oslot=sSCR[3])
                ph = next_ps()
                use_w("tensor", sPS[ph])
                for k in range(ND):
                    mm(PS[ph][:], wtile(wa_, k, 256, sub * 128, P),
                       BIG[:, k * 512:(k + 1) * 512], start=(k == 0),
                       stop=False)
                for k in range(ND):
                    mm(PS[ph][:], wtile(wb_, k, 256, sub * 128, P),
                       BIG[:, (16 + k) * 512:(16 + k + 1) * 512],
                       start=False, stop=False)
                for k in range(12):
                    mm(PS[ph][:], wtile(wc_, k, 256, sub * 128, P),
                       BIG[:, (32 + k) * 512:(32 + k + 1) * 512],
                       start=False, stop=(k == 11),
                       inc="pe" if k == 11 else None)
                mark_w("pe", sPS[ph])
                use_r("vector", sPS[ph], sSCR[2], sSCR[3])
                use_w("vector", sSCR[4])
                em("vector", lambda E, ph=ph: E.tensor_add(
                    SCR[4][:, 0:512], SCR[2][:, 0:512], PS[ph][:]))
                em("vector", lambda E: E.tensor_add(
                    SCR[4][:, 0:512], SCR[4][:, 0:512], SCR[3][:, 0:512]),
                    inc="dv")
                mark_w("dv", sSCR[4])
                mark_r("dv", sPS[ph])
                sSCR[2].r = ("dv", cnt["dv"])
                sSCR[3].r = ("dv", cnt["dv"])
                dma("scalar", x1_scr[dt * P:(dt + 1) * P, H * 512:(H + 1) * 512],
                    SCR[4][:, 0:512], "da", oslot=sDR["x1"], islots=(sSCR[4],))
                xb_, sxb_ = OS[2 + 2 * (dt % 2)], sOS[2 + 2 * (dt % 2)]
                qb_, sqb_ = OS[3 + 2 * (dt % 2)], sOS[3 + 2 * (dt % 2)]
                use_r("scalar", sSCR[4])
                use_w("scalar", sxb_, sqb_)
                em("scalar", lambda E, x=xb_: E.copy(x[:, 0:512],
                                                     SCR[4][:, 0:512]))
                em("scalar", lambda E, x=xb_, q=qb_: E.activation(
                    q[:, 0:512], x[:, 0:512], AF.Square), inc="ac")
                mark_w("ac", sxb_, sqb_)

                def _stats2(dt=dt, x=xb_, q=qb_, sx=sxb_, sq=sqb_):
                    use_r("tensor", sx, sq)
                    if dt == 0:
                        use_w("tensor", sPRA, sPRB)
                    mm(PRA[0:1, :], ONC[:], x[:, 0:512], start=(dt == 0),
                       stop=(dt == ND - 1))
                    mm(PRB[0:1, :], ONC[:], q[:, 0:512], start=(dt == 0),
                       stop=(dt == ND - 1), inc="pe")
                    mark_r("pe", sx, sq)
                pend2.append(_stats2)
                if len(pend2) > 1:
                    pend2.pop(0)()
            sW[wa_].r = ("pe", cnt["pe"])
            sW[wb_].r = ("pe", cnt["pe"])
            sW[wc_].r = ("pe", cnt["pe"])
        while pend2:
            pend2.pop(0)()
        mark_w("pe", sPRA, sPRB)
        use_r("vector", sPRA, sPRB)
        use_w("vector", sROW)
        em("vector", lambda E, H=H: E.tensor_scalar(
            ROW[0:1, 2048 + H * 512:2048 + (H + 1) * 512], PRA[0:1, :],
            1.0 / D, None, ALU.mult))
        em("vector", lambda E: E.tensor_scalar(
            ROW[0:1, 0:512], PRB[0:1, :], 1.0 / D, None, ALU.mult))
        em("vector", lambda E, H=H: E.tensor_tensor(
            ROW[0:1, 512:1024], ROW[0:1, 2048 + H * 512:2048 + (H + 1) * 512],
            ROW[0:1, 2048 + H * 512:2048 + (H + 1) * 512], ALU.mult))
        em("vector", lambda E: E.tensor_sub(
            ROW[0:1, 0:512], ROW[0:1, 0:512], ROW[0:1, 512:1024]))
        em("vector", lambda E: E.tensor_scalar(
            ROW[0:1, 0:512], ROW[0:1, 0:512], 1e-5, None, ALU.add), inc="dv")
        mark_w("dv", sROW)
        mark_r("dv", sPRA, sPRB)
        row_rstd(0, 3072 + H * 512)

    mlp_half(0)
    if sDR["wqp"].w is not None:
        wt("gpsimd", *sDR["wqp"].w)
    em("gpsimd", lambda E: E.collective_compute(
        "AllGather", ALU.bypass, replica_groups=[list(range(NCORE))],
        ins=[wqp_st.opt()], outs=[g2p.opt()]), inc="gp")
    cc2 = cnt["gp"]
    barrier()
    mlp_half(1)
    barrier()

    def b2tile(t):
        if t < 8:
            return OSC[:, t * 512:(t + 1) * 512]
        if t < 12:
            return QG[:, (t - 8) * 512:(t - 7) * 512]
        return KG[:, (t - 12) * 512:(t - 11) * 512]

    wt("sync", "gp", cc2)
    wt("gpsimd", "gp", cc2)
    for H in range(2):
        pend = []
        expand_row(ROW[0:1, 2048 + H * 512:2048 + (H + 1) * 512], 512,
                   BC[1][:, 0:512], sBC[1], 4)
        expand_row(ROW[0:1, 3072 + H * 512:3072 + (H + 1) * 512], 512,
                   BC[2][:, 0:512], sBC[2], 5)
        for t in range(ND):
            dma("sync", SCR[t % 2][:, 0:512],
                x1_scr[t * P:(t + 1) * P, H * 512:(H + 1) * 512], "da",
                oslot=sSCR[t % 2], islots=(sDR["x1"],))
            use_r("vector", sSCR[t % 2], sBC[1])
            b2s = sOSC if t < 8 else (sQG if t < 12 else sKG)
            use_w("vector", b2s)
            em("vector", lambda E, t=t: E.tensor_sub(
                b2tile(t), SCR[t % 2][:, 0:512], BC[1][:, 0:512]), inc="dv")
            mark_w("dv", b2s)
            sSCR[t % 2].r = ("dv", cnt["dv"])
        for pp in range(32):
            bg = pp // 8
            c0 = (pp * 256) % 2048
            wg_, wu_ = next_w(), next_w()
            load_panel(wg_, g2gu, bg * D, (bg + 1) * D, c0, c0 + 256, "sync")
            load_panel(wu_, g2gu, (4 + bg) * D, (5 + bg) * D, c0, c0 + 256,
                       "sync")
            use_r("tensor", sW[wg_], sW[wu_], sQG, sKG, sOSC)
            for sub in range(2):
                ht = pp * 2 + sub
                pg_, pu_ = next_ps(), next_ps()
                for ww, pb in ((wg_, pg_), (wu_, pu_)):
                    use_w("tensor", sPS[pb])
                    for k in range(ND):
                        mm(PS[pb][:], wtile(ww, k, 256, sub * 128, P),
                           b2tile(k), start=(k == 0), stop=(k == ND - 1),
                           inc="pe" if k == ND - 1 else None)
                    mark_w("pe", sPS[pb])
                use_r("vector", sPS[pg_], sPS[pu_], sBC[2])
                use_w("vector", sSCR[2], sSCR[3])
                em("vector", lambda E, pg_=pg_: E.tensor_tensor(
                    SCR[2][:, 0:512], PS[pg_][:], BC[2][:, 0:512], ALU.mult))
                em("vector", lambda E, pu_=pu_: E.tensor_tensor(
                    SCR[3][:, 0:512], PS[pu_][:], BC[2][:, 0:512], ALU.mult),
                    inc="dv")
                mark_w("dv", sSCR[2], sSCR[3])
                mark_r("dv", sPS[pg_], sPS[pu_])
                use_r("scalar", sSCR[2])
                use_w("scalar", sOS[0])
                em("scalar", lambda E: E.activation(
                    OS[0][:, 0:512], SCR[2][:, 0:512], AF.Silu), inc="ac")
                mark_w("ac", sOS[0])
                sSCR[2].r = ("ac", cnt["ac"])
                use_r("vector", sOS[0], sSCR[3])
                use_w("vector", sBIG)
                em("vector", lambda E, ht=ht: E.tensor_tensor(
                    BIG[:, ht * 512:(ht + 1) * 512], OS[0][:, 0:512],
                    SCR[3][:, 0:512], ALU.mult), inc="dv")
                mark_w("dv", sBIG)
                sOS[0].r = ("dv", cnt["dv"])
                sSCR[3].r = ("dv", cnt["dv"])
                sqb, ssqb = OS[2 + ht % 2], sOS[2 + ht % 2]
                use_r("scalar", sBIG)
                use_w("scalar", ssqb)
                em("scalar", lambda E, ht=ht, q=sqb: E.activation(
                    q[:, 0:512], BIG[:, ht * 512:(ht + 1) * 512],
                    AF.Square), inc="ac")
                mark_w("ac", ssqb)

                def _stats(ht=ht, q=sqb, sq=ssqb):
                    use_r("tensor", sq)
                    if ht == 0:
                        use_w("tensor", sPRA)
                    mm(PRA[0:1, :], ONC[:], q[:, 0:512], start=(ht == 0),
                       stop=(ht == NHF - 1), inc="pe")
                    mark_r("pe", sq)
                pend.append(_stats)
                if len(pend) > 1:
                    pend.pop(0)()
            sW[wg_].r = ("pe", cnt["pe"])
            sW[wu_].r = ("pe", cnt["pe"])
        while pend:
            pend.pop(0)()
        mark_w("pe", sPRA)
        use_r("vector", sPRA)
        use_w("vector", sROW)
        em("vector", lambda E: E.tensor_scalar(
            ROW[0:1, 0:512], PRA[0:1, :], 1.0 / HF, 1.1920929e-7, ALU.mult,
            ALU.add), inc="dv")
        mark_w("dv", sROW)
        mark_r("dv", sPRA)
        row_rstd(0, 512)
        expand_row(ROW[0:1, 512:1024], 512, BC[1][:, 0:512], sBC[1], 4)
        for b in range(8):
            wsl4 = [next_w() for _ in range(4)]
            for r4 in range(4):
                load_panel(wsl4[r4], g2p, b * HF + r4 * D, b * HF + (r4 + 1) * D,
                           0, 256, "sync")
            use_r("tensor", *[sW[i] for i in wsl4], sBIG)
            for dsub in range(2):
                dt = b * 2 + dsub
                ph = next_ps()
                use_w("tensor", sPS[ph])
                for r4 in range(4):
                    for k in range(ND):
                        mm(PS[ph][:], wtile(wsl4[r4], k, 256, dsub * 128, P),
                           BIG[:, (r4 * 16 + k) * 512:(r4 * 16 + k + 1) * 512],
                           start=(r4 == 0 and k == 0),
                           stop=(r4 == 3 and k == ND - 1),
                           inc="pe" if (r4 == 3 and k == ND - 1) else None)
                mark_w("pe", sPS[ph])
                dma("sync", SCR[2][:, 0:512],
                    x1_scr[dt * P:(dt + 1) * P, H * 512:(H + 1) * 512], "da",
                    oslot=sSCR[2], islots=(sDR["x1"],))
                use_r("vector", sPS[ph], sSCR[2], sBC[1])
                use_w("vector", sSCR[4])
                em("vector", lambda E, ph=ph: E.tensor_tensor(
                    SCR[4][:, 0:512], PS[ph][:], BC[1][:, 0:512], ALU.mult))
                em("vector", lambda E: E.tensor_add(
                    SCR[4][:, 0:512], SCR[4][:, 0:512], SCR[2][:, 0:512]),
                    inc="dv")
                mark_w("dv", sSCR[4])
                mark_r("dv", sPS[ph])
                sSCR[2].r = ("dv", cnt["dv"])
                dma("sync", y_out[dt * P:(dt + 1) * P, H * 512:(H + 1) * 512],
                    SCR[4][:, 0:512], "out", islots=(sSCR[4],))
            for i in wsl4:
                sW[i].r = ("pe", cnt["pe"])
        barrier()

    if debug:
        sloc_dbg = dram("sloc_dbg", [2 * NH * P, DV], BF16, "ExternalOutput").ap()
        wqgu_dbg = dram("wqgu_dbg", [D, D], BF16, "ExternalOutput").ap()
        dma("sync", sloc_dbg, sloc_scr, "out", islots=(sDR["sloc"],))
        dma("sync", wqgu_dbg, wqgu_st, "out", islots=(sDR["wqgu"],))

    wt("sync", "out", cnt["out"])

    from contextlib import ExitStack as ES2
    with es:
        with ES2() as es2:
            sem_handles = {n: es2.enter_context(nc.semaphore(f"sem_{n}"))
                           for n in cnt.keys()}
            with nc.Block() as block:
                def mk(engname):
                    def body(eng):
                        for it in Q[engname]:
                            if it[0] == "wait":
                                eng.wait_ge(sem_handles[it[1]], it[2])
                            else:
                                inst = it[1](eng)
                                if it[2] is not None:
                                    inst.then_inc(sem_handles[it[2]], it[3])
                    body.__name__ = engname
                    return body
                block.sync(mk("sync"))
                block.scalar(mk("scalar"))
                block.vector(mk("vector"))
                block.tensor(mk("tensor"))
                block.gpsimd(mk("gpsimd"))
    return nc


def _host_tables():
    i_idx = np.arange(CH, dtype=np.float32)
    p_idx = np.arange(T, dtype=np.float32)
    dmaskT = np.zeros((P, NH * CH), dtype=bfnp)
    cdecT = np.zeros((P, NH * CH), dtype=bfnp)
    kdec = np.zeros((P, NH), dtype=np.float32)
    gamtab = np.zeros((NH * P, T), dtype=bfnp)
    for h in range(NH):
        g = np.float32(GAMMA[h])
        jj, ii = i_idx[:, None], i_idx[None, :]
        m = np.where(ii >= jj, g ** (ii - jj), 0.0).astype(np.float32)
        dmaskT[:, h * CH:(h + 1) * CH] = m.astype(bfnp)
        cd = (g ** (i_idx + 1.0)).astype(np.float32)
        cdecT[:, h * CH:(h + 1) * CH] = \
            np.broadcast_to(cd[None, :], (P, CH)).astype(bfnp)
        kdec[:, h] = (g ** (CH - 1.0 - i_idx)).astype(np.float32)
        gt = (g ** (p_idx + 1.0)).astype(np.float32)
        gamtab[h * P:(h + 1) * P, :] = \
            np.broadcast_to(gt[None, :], (P, T)).astype(bfnp)
    return dmaskT, cdecT, kdec, gamtab


def kernel(L0_attn_hidden_states, L0_attn_attn_norm_w, L0_attn_q_w, L0_attn_k_w,
           L0_attn_v_w, L0_attn_g_w, L0_attn_o_w, L0_attn_gnorm_w,
           L0_attn_mlp_norm_w, L0_attn_gate_w, L0_attn_down_w,
           L0_ffn_w_gate_up, L0_ffn_w_proj):
    x = np.asarray(L0_attn_hidden_states, np.float32)
    qw = np.ascontiguousarray(np.asarray(L0_attn_q_w, np.float32))
    kw = np.ascontiguousarray(np.asarray(L0_attn_k_w, np.float32))
    vw = np.ascontiguousarray(np.asarray(L0_attn_v_w, np.float32))
    gw = np.ascontiguousarray(np.asarray(L0_attn_g_w, np.float32))
    ow = np.ascontiguousarray(np.asarray(L0_attn_o_w, np.float32))
    gatew = np.ascontiguousarray(np.asarray(L0_attn_gate_w, np.float32))
    downw = np.ascontiguousarray(np.asarray(L0_attn_down_w, np.float32))
    wguT = np.ascontiguousarray(np.asarray(L0_ffn_w_gate_up, np.float32).T)
    wpT = np.ascontiguousarray(np.asarray(L0_ffn_w_proj, np.float32).T)
    attw = np.ascontiguousarray(
        np.asarray(L0_attn_attn_norm_w, np.float32).reshape(ND, P).T)
    mlpw = np.ascontiguousarray(
        np.asarray(L0_attn_mlp_norm_w, np.float32).reshape(ND, P).T)
    gnw = np.ascontiguousarray(
        np.asarray(L0_attn_gnorm_w, np.float32).reshape(4, P).T)

    dmaskT, cdecT, kdec, gamtab = _host_tables()
    eye = np.eye(P, dtype=bfnp)
    onesb = np.zeros((P, 2), dtype=bfnp)
    onesb[0:64, 0] = 1
    onesb[64:128, 1] = 1
    onesc = np.ones((P, 1), dtype=bfnp)
    onesr = np.ones((1, P), dtype=bfnp)
    inv_freq = (1.0 / (10000.0 ** (np.arange(0, 256, 2, dtype=np.float32)
                                   / np.float32(256)))).astype(np.float32)

    xt = np.ascontiguousarray(x.transpose(0, 2, 1))

    nc = build_program(debug=DEBUG_BUILD)
    in_maps = []
    for c in range(NCORE):
        b, qtr = c // 4, c % 4
        pos = np.arange(qtr * T, (qtr + 1) * T, dtype=np.float32)
        ang = pos[None, :] * inv_freq[:, None]
        wcomb = np.zeros((P, 64), dtype=np.float32)
        for j in range(qtr):
            for h in range(NH):
                wcomb[:, j * 8 + h] = \
                    np.float32(GAMMA[h]) ** np.float32(T * (qtr - 1 - j))
        in_maps.append(dict(
            x0fm=np.ascontiguousarray(xt[b, :, qtr * T:(qtr + 1) * T]),
            qw=qw, kw=kw, vw=vw, gw=gw, ow=ow, gatew=gatew, downw=downw,
            wgu_sh=np.ascontiguousarray(wguT[:, c * D:(c + 1) * D]),
            wp_sh=np.ascontiguousarray(wpT[:, c * 256:(c + 1) * 256]),
            cosraw=np.cos(ang).astype(np.float32),
            sinraw=np.sin(ang).astype(np.float32),
            attw_i=attw, mlpw_i=mlpw, gnw_i=gnw,
            dmaskT_i=dmaskT, cdecT_i=cdecT, kdec_i=kdec, gamtab_i=gamtab,
            wcomb_i=wcomb, eye_i=eye, onesb_i=onesb, onesc_i=onesc,
            onesr_i=onesr,
        ))
    res = run_bass_kernel_spmd(nc, in_maps, core_ids=list(range(NCORE)))
    global _last_results
    _last_results = res.results
    out = np.empty((2, 4096, D), np.float32)
    for c in range(NCORE):
        b, qtr = c // 4, c % 4
        out[b, qtr * T:(qtr + 1) * T, :] = np.asarray(res.results[c]["y"]).T
    return out


if __name__ == "__main__":
    import reference as R
    ins = {k: np.asarray(v) for k, v in R.setup_inputs(0).items()}
    got = kernel(**ins)
    ref = np.asarray(R.reference(**ins), np.float32)
    err = got - ref
    rvr = float(np.mean(err ** 2) / np.mean(ref.astype(np.float64) ** 2))
    print(f"residual-variance ratio = {rvr:.3e} (bar 1.40e-04)")
    bd = np.abs(got.astype(bfnp).astype(np.float32)
                - ref.astype(bfnp).astype(np.float32)).max()
    print(f"max bf16 diff = {bd:.4f} (bar 0.422)")
